# version 13
# baseline (speedup 1.0000x reference)
import numpy as np

import concourse.bass as bass
import concourse.mybir as mybir
from concourse.bass_utils import run_bass_kernel_spmd

F32 = mybir.dt.float32
BF16 = mybir.dt.bfloat16
I32 = mybir.dt.int32
AF = mybir.ActivationFunctionType
ALU = mybir.AluOpType
AX = mybir.AxisListType

T, D, FH, HH = 4096, 1024, 4096, 3072
DC = D // 128
FC = FH // 128
HC = HH // 128
S127 = 1.0 / 127.0
EPS = 1e-5

DEBUG_PHASES = 99
DEBUG_OUTS = ()

ENGS = ("sync", "gpsimd", "tensor", "vector", "scalar")


class Sem:
    def __init__(self, name, eng=None):
        self.name, self.n, self.h, self.eng = name, 0, None, eng


class Buf:

    def __init__(self, ap, name=""):
        self.ap = ap
        self.name = name
        self.w = None
        self.readers = []


class Builder:
    def __init__(self, nc):
        self.nc = nc
        self.ops = {e: [] for e in ENGS}
        self.sems = {e: Sem("sem_" + e, e) for e in ENGS}
        self.dsems = {"sync": Sem("d_sync"), "gpsimd": Sem("d_gpsimd"),
                      "scalar": Sem("d_scalar")}
        self.waited = {e: {} for e in ENGS}
        self.force_drain = True

    def all_sems(self):
        return list(self.sems.values()) + list(self.dsems.values())

    def _wait(self, eng, tok):
        sem, val, src, flag = tok
        if src == eng:
            return
        prev = self.waited[eng].get(sem.name, -1)
        if val <= prev:
            return
        self.waited[eng][sem.name] = val
        flag["x"] = True
        self.ops[eng].append(lambda e, s=sem, v=val: e.wait_ge(s.h, v))

    def op(self, eng, build, reads=(), writes=(), dma=False):
        if dma:
            for b in list(reads) + list(writes):
                if b.w is not None and b.w[2] == eng:
                    b.w[3]["x"] = True
                for t in b.readers:
                    if t[2] == eng:
                        t[3]["x"] = True
        for b in reads:
            if b.w is not None:
                self._wait(eng, b.w)
        for b in writes:
            if b.w is not None:
                self._wait(eng, b.w)
            for t in b.readers:
                self._wait(eng, t)
        if dma:
            sem = self.dsems[eng]
            sem.n += 16
            step = 16
        else:
            sem = self.sems[eng]
            sem.n += 1
            step = 1
        val = sem.n
        flag = {"x": self.force_drain}
        tok = (sem, val, None if dma else eng, flag)

        if dma or eng == "sync":
            def fn(e, build=build, sem=sem, step=step):
                ins = build(e)
                ins.then_inc(sem.h, step)
        else:
            def fn(e, build=build, sem=sem, step=step, flag=flag):
                ins = build(e)
                if flag["x"]:
                    e.drain().then_inc(sem.h, step)
                else:
                    ins.then_inc(sem.h, step)

        self.ops[eng].append(fn)
        for b in reads:
            b.readers.append(tok)
        for b in writes:
            b.w = tok
            b.readers = []
        return tok

    def raw(self, eng, fn):
        self.ops[eng].append(fn)

    def barrier(self):
        toks = []
        for e in ("gpsimd", "tensor", "vector", "scalar"):
            sem = self.sems[e]
            sem.n += 1
            self.ops[e].append(lambda eng, s=sem: eng.drain().then_inc(s.h, 1))
            toks.append((sem, sem.n, e, {"x": True}))
        for ds in self.dsems.values():
            toks.append((ds, ds.n, None, {"x": True}))
        for e in ENGS:
            for t in toks:
                self._wait(e, t)


class Arena:

    def __init__(self, sb, cols):
        self.sb, self.cols, self.top, self.marks = sb, cols, 0, []

    def mark(self):
        self.marks.append(self.top)

    def release(self):
        self.top = self.marks.pop()

    def alloc(self, ncols, name=""):
        a = (self.top + 7) & ~7
        self.top = a + ncols
        assert self.top <= self.cols, f"arena overflow at {name}: {self.top}>{self.cols}"
        return a

    def f32(self, shape, name=""):
        n = int(np.prod(shape[1:]))
        a = self.alloc(n, name)
        ap = self.sb[:, a:a + n]
        if len(shape) > 2:
            ap = ap.rearrange("p (a b) -> p a b", a=shape[1])
        return Buf(ap, name)

    def bf16(self, shape, name=""):
        n = int(np.prod(shape[1:]))
        a = self.alloc((n + 1) // 2, name)
        ap = self.sb[:, a:a + (n + 1) // 2].bitcast(BF16)
        if len(shape) > 2:
            ap = ap.rearrange("p (a b) -> p a b", a=shape[1])
        return Buf(ap, name)


class PsumPool:
    def __init__(self, ps):
        self.banks = [Buf(ps[:, b, :], f"psum{b}") for b in range(8)]
        self.i = 0

    def take(self):
        b = self.banks[self.i]
        self.i = (self.i + 1) % 8
        return b


def build(debug_phases=99, debug_outs=()):
    nc = bass.Bass("TRN2", target_bir_lowering=False)

    def dram(name, shape, dtype, kind):
        if name in debug_outs:
            kind = "ExternalOutput"
        return nc.dram_tensor(name, shape, dtype, kind=kind).ap()

    x_h = dram("x", [T, D], F32, "ExternalInput")
    wk_h = dram("wk", [D, D], F32, "ExternalInput")
    wv_h = dram("wv", [D, D], F32, "ExternalInput")
    wr_h = dram("wr", [D, D], F32, "ExternalInput")
    wo_h = dram("wo", [D, D], F32, "ExternalInput")
    fwk_h = dram("fwk", [D, FH], F32, "ExternalInput")
    fwr_h = dram("fwr", [D, D], F32, "ExternalInput")
    fwv_h = dram("fwv", [FH, D], F32, "ExternalInput")
    wg_h = dram("wg", [HH, D], I32, "ExternalInput")
    wu_h = dram("wu", [HH, D], I32, "ExternalInput")
    wd_h = dram("wd", [D, HH], I32, "ExternalInput")
    pv_h = {}
    for p in ("ln1w", "ln1b", "ln2w", "ln2b", "tdec", "tfir",
              "tmk", "tmv", "tmr", "ftmk", "ftmr"):
        pv_h[p] = dram(p, [D], F32, "ExternalInput")
    out_h = dram("out", [T, D], F32, "ExternalOutput")
    dbg_h = dram("dbg", [128, 16384], F32, "Internal")

    k_fm = dram("k_fm", [D, T], F32, "Internal")
    v_fm = dram("v_fm", [D, T], F32, "Internal")
    r_fm = dram("r_fm", [D, T], F32, "Internal")
    y_fm = dram("y_fm", [D, T], BF16, "Internal")
    s_tm = dram("s_tm", [T, D], F32, "Internal")
    fk2_fm = dram("fk2_fm", [FH, T], BF16, "Internal")
    sfr_tm = dram("sfr_tm", [T, D], BF16, "Internal")
    y1_tm = dram("y1_tm", [T, D], F32, "Internal")
    y1_fm = dram("y1_fm", [D, T], BF16, "Internal")
    wg_bf = dram("wg_bf", [HH, D], BF16, "Internal")
    wu_bf = dram("wu_bf", [HH, D], BF16, "Internal")
    wd_bf = dram("wd_bf", [D, HH], BF16, "Internal")
    HB = {n: Buf(None, "HB_" + n) for n in
          ("k_fm", "v_fm", "r_fm", "y_fm", "s_tm", "fk2_fm", "sfr_tm",
           "y1_tm", "y1_fm", "wg_bf", "wu_bf", "wd_bf")}

    ARENA_COLS = 52224
    import contextlib
    stack = contextlib.ExitStack()
    sb = stack.enter_context(nc.sbuf_tensor("arena", [128, ARENA_COLS], F32))
    ps = stack.enter_context(nc.psum_tensor("psum", [128, 8, 512], F32))

    B = Builder(nc)
    for s in B.all_sems():
        s.h = stack.enter_context(nc.semaphore(s.name))

    ar = Arena(sb, ARENA_COLS)
    pool = PsumPool(ps)

    ident = ar.f32([128, 128], "ident")
    P = {p: ar.f32([128, DC], p) for p in pv_h}
    dw = ar.f32([128, DC], "dw")
    eu = ar.f32([128, DC], "eu")
    mhalf = ar.f32([128, 8], "mhalf")
    epsb = ar.f32([128, 1], "epsb")
    stat2 = [ar.f32([128, 32], f"stat{j}") for j in range(2)]
    sq = ar.f32([128, 1024], "sq")

    B.op("gpsimd", lambda e: e.memset(ident.ap, 0.0), writes=[ident])
    B.op("gpsimd", lambda e: e.affine_select(
        out=ident.ap, in_=ident.ap, compare_op=ALU.not_equal, fill=1.0,
        base=0, pattern=[[-1, 128]], channel_multiplier=1),
        reads=[ident], writes=[ident])
    B.op("gpsimd", lambda e: e.memset(mhalf.ap, -0.5), writes=[mhalf])
    B.op("gpsimd", lambda e: e.memset(epsb.ap, EPS), writes=[epsb])
    for p, h in pv_h.items():
        B.op("gpsimd", lambda e, h=h, p=p: e.dma_start(
            P[p].ap, h.rearrange("(c p) -> p c", p=128),
            allow_slow_non_contiguous=True), writes=[P[p]], dma=True)
    B.op("scalar", lambda e: e.activation(dw.ap, P["tdec"].ap, AF.Exp),
         reads=[P["tdec"]], writes=[dw])
    B.op("scalar", lambda e: e.activation(dw.ap, dw.ap, AF.Exp, scale=-1.0),
         reads=[dw], writes=[dw])
    B.op("scalar", lambda e: e.activation(eu.ap, P["tfir"].ap, AF.Exp),
         reads=[P["tfir"]], writes=[eu])

    _dq_jobs = []
    for (src, dsth, dstbn, nch) in ((wg_h, wg_bf, "wg_bf", HC),
                                    (wu_h, wu_bf, "wu_bf", HC),
                                    (wd_h, wd_bf, "wd_bf", DC)):
        for c in range(nch):
            _dq_jobs.append((src, dsth, dstbn, c))

    def emit_ffn_dequant(frac_lo, frac_hi):
        lo = int(len(_dq_jobs) * frac_lo)
        hi = int(len(_dq_jobs) * frac_hi)
        for idx in range(lo, hi):
            src, dsth, dstbn, c = _dq_jobs[idx]
            width = src.shape[1]
            stgb = wstg[idx % 2]
            B.op("gpsimd", lambda e, src=src, c=c, stgb=stgb, width=width:
                 e.dma_start(stgb.ap[:, 0:width], src[c * 128:(c + 1) * 128, :]),
                 writes=[stgb], dma=True)
            B.op("gpsimd", lambda e, dsth=dsth, c=c, stgb=stgb, width=width:
                 e.dma_start(dsth[c * 128:(c + 1) * 128, :], stgb.ap[:, 0:width]),
                 reads=[stgb], writes=[HB[dstbn]], dma=True)

    def ln_stats(xbuf, nsub, stat):
        xa = xbuf.ap

        def reds(e, nsub=nsub, xa=xa, stat=stat):
            for q in range(nsub):
                e.reduce_sum(stat.ap[:, q:q + 1], xa[:, q, :], AX.X)
            return e.nop()
        B.op("vector", reds, reads=[xbuf], writes=[stat])

        def sqs(e, nsub=nsub, xa=xa, stat=stat):
            for q in range(nsub):
                e.activation(sq.ap, xa[:, q, :], AF.Square,
                             accum_out=stat.ap[:, 4 + q:5 + q])
            return e.nop()
        B.op("scalar", sqs, reads=[xbuf], writes=[stat, sq])
        B.op("scalar", lambda e: e.activation(
            stat.ap[:, 8:8 + nsub], stat.ap[:, 0:nsub], AF.Identity, scale=1.0 / D),
            reads=[stat], writes=[stat])
        B.op("scalar", lambda e: e.activation(
            stat.ap[:, 12:12 + nsub], stat.ap[:, 8:8 + nsub], AF.Square),
            reads=[stat], writes=[stat])
        B.op("scalar", lambda e: e.activation(
            stat.ap[:, 28:28 + nsub], stat.ap[:, 4:4 + nsub], AF.Identity,
            scale=1.0 / D, bias=epsb.ap),
            reads=[stat, epsb], writes=[stat])
        B.op("gpsimd", lambda e: e.tensor_tensor(
            stat.ap[:, 16:16 + nsub], stat.ap[:, 28:28 + nsub],
            stat.ap[:, 12:12 + nsub], ALU.subtract),
            reads=[stat], writes=[stat])
        B.op("gpsimd", lambda e: e.tensor_tensor(
            stat.ap[:, 20:20 + nsub], stat.ap[:, 16:16 + nsub],
            mhalf.ap[:, 0:nsub], ALU.pow),
            reads=[stat, mhalf], writes=[stat])
        B.op("scalar", lambda e: e.activation(
            stat.ap[:, 12:12 + nsub], stat.ap[:, 8:8 + nsub], AF.Identity, scale=-1.0),
            reads=[stat], writes=[stat])
        B.op("gpsimd", lambda e: e.tensor_tensor(
            stat.ap[:, 24:24 + nsub], stat.ap[:, 12:12 + nsub],
            stat.ap[:, 20:20 + nsub], ALU.mult),
            reads=[stat, mhalf], writes=[stat])

    def mm_tile(psumb, n, lhsT_buf, lhsT_aps, rhs_buf, rhs_aps):
        mm_jobs([(psumb, lhsT_aps, rhs_aps)], n, [lhsT_buf, rhs_buf])

    def mm_jobs(jobs, n, reads):

        def grp(e):
            last = None
            for psumb, lts, rhs in jobs:
                cnt = len(lts)
                for i in range(cnt):
                    last = e.matmul(psumb.ap[:, 0:n], lts[i], rhs[i],
                                    start=(i == 0), stop=(i == cnt - 1))
            return last
        B.op("tensor", grp, reads=reads, writes=[j[0] for j in jobs])

    def transpose_group(dcs_qs, src_buf, src_aps, bank):

        def grp(e):
            last = None
            for idx, ap in enumerate(src_aps):
                last = e.transpose(bank.ap[:, idx * 128:(idx + 1) * 128], ap, ident.ap)
            return last
        B.op("tensor", grp, reads=[src_buf, ident], writes=[bank])

    ar.mark()
    wstg = [ar.bf16([128, 3072], "wstg0")] * 2
    wk_sb = ar.bf16([128, DC, 1024], "wk_sb")
    wv_sb = ar.bf16([128, DC, 1024], "wv_sb")
    wr_sb = ar.bf16([128, DC, 1024], "wr_sb")
    x_t2 = [ar.f32([128, 4, 1024], f"x_t{j}") for j in range(2)]
    n_t = ar.f32([128, 4, 1024], "n_t")
    xn = ar.f32([128, DC, 513], "xn")
    dnc = ar.f32([128, 512], "dnc")
    m_k2 = [ar.bf16([128, DC, 512], f"m_k{j}") for j in range(2)]
    m_v2 = [ar.bf16([128, DC, 512], f"m_v{j}") for j in range(2)]
    m_r2 = [ar.bf16([128, DC, 512], f"m_r{j}") for j in range(2)]
    stg = {p: [ar.f32([128, 512], f"stg_{p}{j}") for j in range(4)]
           for p in ("k", "v", "r")}

    for w_h, w_sb in ((wk_h, wk_sb), (wv_h, wv_sb), (wr_h, wr_sb)):
        B.op("gpsimd", lambda e, w_h=w_h, w_sb=w_sb: e.dma_start(
            w_sb.ap, w_h.rearrange("(c p) o -> p c o", p=128)),
            writes=[w_sb], dma=True)
    B.op("gpsimd", lambda e: e.memset(xn.ap[:, :, 0:1], 0.0), writes=[xn])

    def emit_projA(i):
        t0 = 512 * i
        m_k, m_v, m_r = m_k2[i % 2], m_v2[i % 2], m_r2[i % 2]
        for name, w_sb, m, dst, hb in (("k", wk_sb, m_k, k_fm, HB["k_fm"]),
                                       ("v", wv_sb, m_v, v_fm, HB["v_fm"]),
                                       ("r", wr_sb, m_r, r_fm, HB["r_fm"])):
            for ob in range(0, DC, 4):
                jobs = []
                for oc in range(ob, ob + 4):
                    pb = pool.take()
                    jobs.append((pb, [w_sb.ap[:, kc, oc * 128:(oc + 1) * 128] for kc in range(DC)],
                                 [m.ap[:, kc, :] for kc in range(DC)]))
                mm_jobs(jobs, 512, [w_sb, m])
                sgs = [stg[name][oc % 4] for oc in range(ob, ob + 4)]

                def evac4(e, jobs=jobs, sgs=sgs):
                    for k4 in range(4):
                        e.activation(sgs[k4].ap, jobs[k4][0].ap, AF.Copy)
                    return e.nop()
                B.op("scalar", evac4, reads=[j[0] for j in jobs], writes=sgs)
                for k4, oc in enumerate(range(ob, ob + 4)):
                    B.op("scalar", lambda e, dst=dst, oc=oc, t0=t0, sg_=sgs[k4]: e.dma_start(
                        dst[oc * 128:(oc + 1) * 128, t0:t0 + 512], sg_.ap),
                        reads=[sgs[k4]], writes=[hb], dma=True)

    if debug_phases >= 1:
        for i in range(8):
            t0 = 512 * i
            x_t = x_t2[i % 2]
            m_k, m_v, m_r = m_k2[i % 2], m_v2[i % 2], m_r2[i % 2]
            stat = stat2[i % 2]
            B.op("sync", lambda e, t0=t0, x_t=x_t: e.dma_start(
                x_t.ap, x_h[t0:t0 + 512, :].rearrange("(q p) d -> p q d", p=128)),
                writes=[x_t], dma=True)
            ln_stats(x_t, 4, stat)
            def napplyA(e, n_t=n_t, x_t=x_t, stat=stat):
                for q in range(4):
                    e.activation(n_t.ap[:, q, :], x_t.ap[:, q, :], AF.Identity,
                                 bias=stat.ap[:, 24 + q:25 + q],
                                 scale=stat.ap[:, 20 + q:21 + q])
                return e.nop()
            B.op("scalar", napplyA, reads=[x_t, stat], writes=[n_t])
            for dc in range(DC):
                pb = pool.take()
                transpose_group(None, n_t,
                                [n_t.ap[:, q, dc * 128:(dc + 1) * 128] for q in range(4)], pb)
                B.op("scalar", lambda e, dc=dc, pb=pb: e.activation(
                    xn.ap[:, dc, 1:513], pb.ap[:, 0:512],
                    AF.Identity, bias=P["ln1b"].ap[:, dc:dc + 1],
                    scale=P["ln1w"].ap[:, dc:dc + 1]),
                    reads=[pb, P["ln1w"], P["ln1b"]], writes=[xn])
            for dc in range(DC):
                B.op("vector", lambda e, dc=dc: e.scalar_tensor_tensor(
                    dnc.ap, xn.ap[:, dc, 1:513], 1.0, xn.ap[:, dc, 0:512],
                    ALU.mult, ALU.subtract),
                    reads=[xn], writes=[dnc])

                def mixes(e, dc=dc, m_k=m_k, m_v=m_v, m_r=m_r):
                    for tm, m in (("tmk", m_k), ("tmv", m_v), ("tmr", m_r)):
                        e.scalar_tensor_tensor(
                            m.ap[:, dc, :], dnc.ap, P[tm].ap[:, dc:dc + 1],
                            xn.ap[:, dc, 0:512], ALU.mult, ALU.add)
                    return e.nop()
                B.op("vector", mixes,
                     reads=[dnc, xn, P["tmk"], P["tmv"], P["tmr"]],
                     writes=[m_k, m_v, m_r])
            B.op("gpsimd", lambda e: e.tensor_copy(xn.ap[:, :, 0:1], xn.ap[:, :, 512:513]),
                 reads=[xn], writes=[xn])
            if i >= 1:
                emit_projA(i - 1)
            emit_ffn_dequant(i / 8.0, (i + 1) / 8.0)
        emit_projA(7)
    ar.release()
    B.barrier()

    ar.mark()
    kb2 = [ar.f32([128, 4096], f"kb{j}") for j in range(2)]
    vb2 = [ar.f32([128, 4096], f"vb{j}") for j in range(2)]
    rb2 = [ar.f32([128, 4096], f"rb{j}") for j in range(2)]
    ek = ar.f32([128, 4096], "ek")
    ekv = ar.f32([128, 4096], "ekv")
    tmp = ar.f32([128, 4096], "tmp")
    Abuf = ar.f32([128, 4100], "Abuf")
    Bbuf = ar.f32([128, 4100], "Bbuf")
    yb2 = [ar.bf16([128, 4096], f"yb{j}") for j in range(2)]
    kmax2 = [ar.f32([128, 2], f"kmax{j}") for j in range(2)]

    B.force_drain = True
    B.op("gpsimd", lambda e: e.memset(Abuf.ap[:, 0:1], 0.0), writes=[Abuf])
    B.op("gpsimd", lambda e: e.memset(Bbuf.ap[:, 0:1], 0.0), writes=[Bbuf])

    if debug_phases >= 2:
        for j in range(DC):
            kb, vb, rb = kb2[j % 2], vb2[j % 2], rb2[j % 2]
            yb, kmax = yb2[j % 2], kmax2[j % 2]
            for src, hb, b in ((k_fm, HB["k_fm"], kb), (v_fm, HB["v_fm"], vb),
                               (r_fm, HB["r_fm"], rb)):
                B.op("sync", lambda e, src=src, j=j, b=b: e.dma_start(
                    b.ap, src[j * 128:(j + 1) * 128, :]),
                    reads=[hb], writes=[b], dma=True)
            if j == 7 and "dbg" in debug_outs:
                B.op("sync", lambda e: e.dma_start(dbg_h[:, 0:4096], rb.ap),
                     reads=[rb], dma=True)
            B.op("scalar", lambda e, rb=rb: e.activation(
                tmp.ap, rb.ap, AF.Exp, scale=-1.0), reads=[rb], writes=[tmp])
            B.op("vector", lambda e, kmax=kmax, kb=kb: e.reduce_max(
                kmax.ap[:, 0:1], kb.ap[:, 0:512], AX.X), reads=[kb], writes=[kmax])
            B.op("vector", lambda e, kmax=kmax: e.tensor_scalar(
                kmax.ap[:, 1:2], kmax.ap[:, 0:1], -1.0, None, ALU.mult),
                reads=[kmax], writes=[kmax])
            B.op("scalar", lambda e, kb=kb, kmax=kmax: e.activation(
                ek.ap, kb.ap, AF.Exp, bias=kmax.ap[:, 1:2]),
                reads=[kb, kmax], writes=[ek])
            B.op("vector", lambda e, vb=vb: e.scalar_tensor_tensor(
                ekv.ap, ek.ap, 1.0, vb.ap, ALU.mult, ALU.mult),
                reads=[ek, vb], writes=[ekv])
            B.op("vector", lambda e, j=j: e.tensor_tensor_scan(
                Abuf.ap[:, 1:4097], dw.ap[:, j:j + 1].to_broadcast([128, 4096]),
                ekv.ap, 0.0, ALU.mult, ALU.add),
                reads=[ekv, dw], writes=[Abuf])
            B.op("vector", lambda e, j=j: e.tensor_tensor_scan(
                Bbuf.ap[:, 1:4097], dw.ap[:, j:j + 1].to_broadcast([128, 4096]),
                ek.ap, 0.0, ALU.mult, ALU.add),
                reads=[ek, dw], writes=[Bbuf])
            B.op("vector", lambda e, j=j, kb=kb: e.scalar_tensor_tensor(
                kb.ap, ekv.ap, eu.ap[:, j:j + 1], Abuf.ap[:, 0:4096],
                ALU.mult, ALU.add),
                reads=[ekv, eu, Abuf], writes=[kb])
            B.op("vector", lambda e, j=j, vb=vb: e.scalar_tensor_tensor(
                vb.ap, ek.ap, eu.ap[:, j:j + 1], Bbuf.ap[:, 0:4096],
                ALU.mult, ALU.add),
                reads=[ek, eu, Bbuf], writes=[vb])
            B.op("vector", lambda e, vb=vb: e.scalar_tensor_tensor(
                ekv.ap, tmp.ap, 1.0, vb.ap, ALU.add, ALU.mult),
                reads=[tmp, vb], writes=[ekv])
            B.op("scalar", lambda e: e.activation(
                tmp.ap, ekv.ap, AF.Ln), reads=[ekv], writes=[tmp])
            B.op("scalar", lambda e: e.activation(
                tmp.ap, tmp.ap, AF.Exp, scale=-1.0), reads=[tmp], writes=[tmp])
            if j == 7 and "dbg" in debug_outs:
                B.op("sync", lambda e: e.dma_start(dbg_h[:, 4096:8192], Abuf.ap[:, 0:4096]),
                     reads=[Abuf], dma=True)
                B.op("sync", lambda e: e.dma_start(dbg_h[:, 8192:12288], kb.ap),
                     reads=[kb], dma=True)
                B.op("sync", lambda e: e.dma_start(dbg_h[:, 12288:12290], kmax.ap),
                     reads=[kmax], dma=True)
                B.op("sync", lambda e: e.dma_start(
                    dbg_h[:, 12296:14344].bitcast(BF16), yb.ap), reads=[yb], dma=True)

            B.op("vector", lambda e, yb=yb, kb=kb: e.scalar_tensor_tensor(
                yb.ap, kb.ap, 1.0, tmp.ap, ALU.mult, ALU.mult),
                reads=[kb, tmp], writes=[yb])
            B.op("scalar", lambda e, j=j, yb=yb: e.dma_start(
                y_fm[j * 128:(j + 1) * 128, :], yb.ap),
                reads=[yb], writes=[HB["y_fm"]], dma=True)
    B.force_drain = False
    ar.release()
    B.barrier()

    ar.mark()
    wo_sb = ar.bf16([128, DC, 1024], "wo_sb")
    fwk_sb = ar.bf16([128, DC, 4096], "fwk_sb")
    fwr_sb = ar.bf16([128, DC, 1024], "fwr_sb")
    xc2 = [ar.f32([128, 2, 1024], f"xc{j}") for j in range(2)]
    yc2 = [ar.bf16([128, DC, 256], f"yc{j}") for j in range(2)]
    x1 = ar.f32([128, 2, 1024], "x1")
    sc = ar.f32([128, 2, 1024], "sc")
    n2 = ar.f32([128, 2, 1024], "n2")
    xn2 = ar.f32([128, DC, 257], "xn2")
    dn2 = ar.f32([128, 256], "dn2")
    m2k2 = [ar.bf16([128, DC, 256], f"m2k{j}") for j in range(2)]
    m2r2 = [ar.bf16([128, DC, 256], f"m2r{j}") for j in range(2)]
    fkst = ar.bf16([128, FC, 256], "fkst")
    relu_t = ar.f32([128, 1024], "relu_t")
    sfrst = ar.bf16([128, 2, 1024], "sfrst")

    if debug_phases >= 3:
        for w_h, w_sb in ((wo_h, wo_sb), (fwk_h, fwk_sb), (fwr_h, fwr_sb)):
            B.op("gpsimd", lambda e, w_h=w_h, w_sb=w_sb: e.dma_start(
                w_sb.ap, w_h.rearrange("(c p) o -> p c o", p=128)),
                writes=[w_sb], dma=True)
        B.op("gpsimd", lambda e: e.memset(xn2.ap[:, :, 0:1], 0.0), writes=[xn2])

        def emit_fkfr(i):
            t0 = 256 * i
            m2k, m2r = m2k2[i % 2], m2r2[i % 2]
            for fb in range(0, FC, 4):
                jobs = []
                for fc in range(fb, fb + 4):
                    pb = pool.take()
                    jobs.append((pb, [fwk_sb.ap[:, kc, fc * 128:(fc + 1) * 128] for kc in range(DC)],
                                 [m2k.ap[:, kc, :] for kc in range(DC)]))
                mm_jobs(jobs, 256, [fwk_sb, m2k])

                def relu4(e, jobs=jobs):
                    for k4 in range(4):
                        e.tensor_scalar(relu_t.ap[:, k4 * 256:(k4 + 1) * 256],
                                        jobs[k4][0].ap[:, 0:256], 0.0, None, ALU.max)
                    return e.nop()
                B.op("vector", relu4, reads=[j[0] for j in jobs], writes=[relu_t])
                B.op("scalar", lambda e, fb=fb, fkst=fkst: e.activation(
                    fkst.ap[:, fb:fb + 4, :].rearrange("p a b -> p (a b)"),
                    relu_t.ap, AF.Square),
                    reads=[relu_t], writes=[fkst])
            B.op("scalar", lambda e, t0=t0, fkst=fkst: e.dma_start(
                fk2_fm.rearrange("(c p) t -> p c t", p=128)[:, :, t0:t0 + 256], fkst.ap),
                reads=[fkst], writes=[HB["fk2_fm"]], dma=True)
            for q in range(2):
                for oh in range(2):
                    pb = pool.take()
                    mm_tile(pb, 512, m2r,
                            [m2r.ap[:, kc, q * 128:(q + 1) * 128] for kc in range(DC)],
                            fwr_sb,
                            [fwr_sb.ap[:, kc, oh * 512:(oh + 1) * 512] for kc in range(DC)])
                    B.op("scalar", lambda e, q=q, oh=oh, pb=pb: e.activation(
                        sfrst.ap[:, q, oh * 512:(oh + 1) * 512], pb.ap, AF.Sigmoid),
                        reads=[pb], writes=[sfrst])
            B.op("scalar", lambda e, t0=t0: e.dma_start(
                sfr_tm[t0:t0 + 256, :].rearrange("(q p) d -> p q d", p=128), sfrst.ap),
                reads=[sfrst], writes=[HB["sfr_tm"]], dma=True)

        for i in range(16):
            t0 = 256 * i
            xc, yc = xc2[i % 2], yc2[i % 2]
            m2k, m2r = m2k2[i % 2], m2r2[i % 2]
            stat = stat2[i % 2]
            B.op("sync", lambda e, t0=t0, xc=xc: e.dma_start(
                xc.ap, x_h[t0:t0 + 256, :].rearrange("(q p) d -> p q d", p=128)),
                writes=[xc], dma=True)
            B.op("sync", lambda e, t0=t0, yc=yc: e.dma_start(
                yc.ap, y_fm.rearrange("(c p) t -> p c t", p=128)[:, :, t0:t0 + 256]),
                reads=[HB["y_fm"]], writes=[yc], dma=True)
            jobs = []
            for q in range(2):
                for oh in range(2):
                    pb = pool.take()
                    jobs.append((pb, [yc.ap[:, kc, q * 128:(q + 1) * 128] for kc in range(DC)],
                                 [wo_sb.ap[:, kc, oh * 512:(oh + 1) * 512] for kc in range(DC)]))
            mm_jobs(jobs, 512, [yc, wo_sb])
            ji = 0
            for q in range(2):
                for oh in range(2):
                    pb = jobs[ji][0]; ji += 1
                    B.op("vector", lambda e, q=q, oh=oh, pb=pb, xc=xc: e.scalar_tensor_tensor(
                        x1.ap[:, q, oh * 512:(oh + 1) * 512], pb.ap, 1.0,
                        xc.ap[:, q, oh * 512:(oh + 1) * 512], ALU.mult, ALU.add),
                        reads=[pb, xc], writes=[x1])
            for q in range(2):
                B.op("vector", lambda e, q=q, xc=xc: e.scalar_tensor_tensor(
                    sc.ap[:, q, :], x1.ap[:, q, :], 1.0, xc.ap[:, q, :],
                    ALU.mult, ALU.add), reads=[x1, xc], writes=[sc])
            B.op("scalar", lambda e, t0=t0: e.dma_start(
                s_tm[t0:t0 + 256, :].rearrange("(q p) d -> p q d", p=128), sc.ap),
                reads=[sc], writes=[HB["s_tm"]], dma=True)
            ln_stats(x1, 2, stat)
            def napplyC(e, stat=stat):
                for q in range(2):
                    e.activation(n2.ap[:, q, :], x1.ap[:, q, :], AF.Identity,
                                 bias=stat.ap[:, 24 + q:25 + q],
                                 scale=stat.ap[:, 20 + q:21 + q])
                return e.nop()
            B.op("scalar", napplyC, reads=[x1, stat], writes=[n2])
            if i >= 1:
                emit_fkfr(i - 1)
            for dc in range(DC):
                pb = pool.take()
                transpose_group(None, n2,
                                [n2.ap[:, q, dc * 128:(dc + 1) * 128] for q in range(2)], pb)
                B.op("scalar", lambda e, dc=dc, pb=pb: e.activation(
                    xn2.ap[:, dc, 1:257], pb.ap[:, 0:256],
                    AF.Identity, bias=P["ln2b"].ap[:, dc:dc + 1],
                    scale=P["ln2w"].ap[:, dc:dc + 1]),
                    reads=[pb, P["ln2w"], P["ln2b"]], writes=[xn2])
            for dc in range(DC):
                B.op("vector", lambda e, dc=dc: e.scalar_tensor_tensor(
                    dn2.ap, xn2.ap[:, dc, 1:257], 1.0, xn2.ap[:, dc, 0:256],
                    ALU.mult, ALU.subtract), reads=[xn2], writes=[dn2])

                def mixes2(e, dc=dc, m2k=m2k, m2r=m2r):
                    for tm, m in (("ftmk", m2k), ("ftmr", m2r)):
                        e.scalar_tensor_tensor(
                            m.ap[:, dc, :], dn2.ap, P[tm].ap[:, dc:dc + 1],
                            xn2.ap[:, dc, 0:256], ALU.mult, ALU.add)
                    return e.nop()
                B.op("vector", mixes2, reads=[dn2, xn2, P["ftmk"], P["ftmr"]],
                     writes=[m2k, m2r])
            B.op("gpsimd", lambda e: e.tensor_copy(xn2.ap[:, :, 0:1], xn2.ap[:, :, 256:257]),
                 reads=[xn2], writes=[xn2])
        emit_fkfr(15)
    ar.release()
    B.barrier()

    ar.mark()
    fwv_sb = ar.bf16([128, FC, 1024], "fwv_sb")
    fk2c2 = [ar.bf16([128, FC, 512], f"fk2c{j}") for j in range(2)]
    sfrc = ar.bf16([128, 4, 1024], "sfrc")
    scc = ar.f32([128, 4, 1024], "scc")
    y1p = ar.f32([128, 4, 1024], "y1p")
    y1t = ar.f32([128, 4, 1024], "y1t")
    y1f = ar.bf16([128, DC, 512], "y1f")

    if debug_phases >= 4:
        B.op("gpsimd", lambda e: e.dma_start(
            fwv_sb.ap, fwv_h.rearrange("(c p) o -> p c o", p=128)),
            writes=[fwv_sb], dma=True)
        for i in range(8):
            t0 = 512 * i
            fk2c = fk2c2[i % 2]
            stat = stat2[i % 2]
            B.op("sync", lambda e, t0=t0, fk2c=fk2c: e.dma_start(
                fk2c.ap, fk2_fm.rearrange("(c p) t -> p c t", p=128)[:, :, t0:t0 + 512]),
                reads=[HB["fk2_fm"]], writes=[fk2c], dma=True)
            B.op("sync", lambda e, t0=t0: e.dma_start(
                sfrc.ap, sfr_tm[t0:t0 + 512, :].rearrange("(q p) d -> p q d", p=128)),
                reads=[HB["sfr_tm"]], writes=[sfrc], dma=True)
            B.op("sync", lambda e, t0=t0: e.dma_start(
                scc.ap, s_tm[t0:t0 + 512, :].rearrange("(q p) d -> p q d", p=128)),
                reads=[HB["s_tm"]], writes=[scc], dma=True)
            for q in range(4):
                jobs = []
                for oh in range(2):
                    pb = pool.take()
                    jobs.append((pb, [fk2c.ap[:, fc, q * 128:(q + 1) * 128] for fc in range(FC)],
                                 [fwv_sb.ap[:, fc, oh * 512:(oh + 1) * 512] for fc in range(FC)]))
                mm_jobs(jobs, 512, [fk2c, fwv_sb])
                for oh in range(2):
                    pb = jobs[oh][0]
                    B.op("vector", lambda e, q=q, oh=oh, pb=pb: e.scalar_tensor_tensor(
                        y1p.ap[:, q, oh * 512:(oh + 1) * 512], pb.ap, 1.0,
                        sfrc.ap[:, q, oh * 512:(oh + 1) * 512], ALU.mult, ALU.mult),
                        reads=[pb, sfrc], writes=[y1p])
                    B.op("vector", lambda e, q=q, oh=oh: e.scalar_tensor_tensor(
                        y1p.ap[:, q, oh * 512:(oh + 1) * 512],
                        y1p.ap[:, q, oh * 512:(oh + 1) * 512], 1.0,
                        scc.ap[:, q, oh * 512:(oh + 1) * 512], ALU.mult, ALU.add),
                        reads=[y1p, scc], writes=[y1p])
            ln_stats(y1p, 4, stat)
            def napplyY(e, stat=stat):
                for q in range(4):
                    e.activation(y1t.ap[:, q, :], y1p.ap[:, q, :], AF.Identity,
                                 bias=stat.ap[:, 24 + q:25 + q],
                                 scale=stat.ap[:, 20 + q:21 + q])
                return e.nop()
            B.op("scalar", napplyY, reads=[y1p, stat], writes=[y1t])
            B.op("scalar", lambda e, t0=t0: e.dma_start(
                y1_tm[t0:t0 + 512, :].rearrange("(q p) d -> p q d", p=128), y1t.ap),
                reads=[y1t], writes=[HB["y1_tm"]], dma=True)
            for dc in range(DC):
                pb = pool.take()
                transpose_group(None, y1t,
                                [y1t.ap[:, q, dc * 128:(dc + 1) * 128] for q in range(4)], pb)
                B.op("scalar", lambda e, dc=dc, pb=pb: e.activation(
                    y1f.ap[:, dc, :], pb.ap[:, 0:512], AF.Copy),
                    reads=[pb], writes=[y1f])
            B.op("scalar", lambda e, t0=t0: e.dma_start(
                y1_fm.rearrange("(c p) t -> p c t", p=128)[:, :, t0:t0 + 512], y1f.ap),
                reads=[y1f], writes=[HB["y1_fm"]], dma=True)
    ar.release()
    B.barrier()

    ar.mark()
    wgT = ar.bf16([128, DC, 3072], "wgT")
    wuT = ar.bf16([128, DC, 3072], "wuT")
    wdT = ar.bf16([128, HC, 1024], "wdT")
    y1fc = ar.bf16([128, DC, 256], "y1fc")
    y1tc = ar.f32([128, 2, 1024], "y1tc")
    sg = ar.bf16([128, HC, 256], "sg")
    up = ar.bf16([128, HC, 256], "up")
    y2p = ar.f32([128, 2, 1024], "y2p")

    if debug_phases >= 5:
        for (srch, hb, dstb, nch) in ((wg_bf, HB["wg_bf"], wgT, DC),
                                      (wu_bf, HB["wu_bf"], wuT, DC),
                                      (wd_bf, HB["wd_bf"], wdT, HC)):
            for c in range(nch):
                B.op("scalar", lambda e, srch=srch, dstb=dstb, c=c: e.dma_start_transpose(
                    dstb.ap[:, c, :], srch[:, c * 128:(c + 1) * 128]),
                    reads=[hb], writes=[dstb], dma=True)
        for i in range(16):
            t0 = 256 * i
            stat = stat2[i % 2]
            B.op("sync", lambda e, t0=t0, y1fc=y1fc: e.dma_start(
                y1fc.ap, y1_fm.rearrange("(c p) t -> p c t", p=128)[:, :, t0:t0 + 256]),
                reads=[HB["y1_fm"]], writes=[y1fc], dma=True)
            B.op("sync", lambda e, t0=t0: e.dma_start(
                y1tc.ap, y1_tm[t0:t0 + 256, :].rearrange("(q p) d -> p q d", p=128)),
                reads=[HB["y1_tm"]], writes=[y1tc], dma=True)
            for hb2 in range(0, HC, 2):
                jobs = []
                for hc in (hb2, hb2 + 1):
                    pb = pool.take()
                    jobs.append((pb, [wgT.ap[:, kc, hc * 128:(hc + 1) * 128] for kc in range(DC)],
                                 [y1fc.ap[:, kc, :] for kc in range(DC)]))
                for hc in (hb2, hb2 + 1):
                    pb = pool.take()
                    jobs.append((pb, [wuT.ap[:, kc, hc * 128:(hc + 1) * 128] for kc in range(DC)],
                                 [y1fc.ap[:, kc, :] for kc in range(DC)]))
                mm_jobs(jobs, 256, [wgT, wuT, y1fc])
                for k2, hc in enumerate((hb2, hb2 + 1)):
                    pb = jobs[k2][0]
                    B.op("scalar", lambda e, hc=hc, pb=pb: e.activation(
                        sg.ap[:, hc, :], pb.ap[:, 0:256], AF.Silu, scale=S127),
                        reads=[pb], writes=[sg])
                for k2, hc in enumerate((hb2, hb2 + 1)):
                    pb2 = jobs[2 + k2][0]
                    B.op("vector", lambda e, hc=hc, pb2=pb2, up=up: e.scalar_tensor_tensor(
                        up.ap[:, hc, :], pb2.ap[:, 0:256], 1.0, sg.ap[:, hc, :],
                        ALU.mult, ALU.mult), reads=[pb2, sg], writes=[up])
            jobs = []
            for q in range(2):
                for oh in range(2):
                    pb = pool.take()
                    jobs.append((pb, [up.ap[:, hc, q * 128:(q + 1) * 128] for hc in range(HC)],
                                 [wdT.ap[:, hc, oh * 512:(oh + 1) * 512] for hc in range(HC)]))
            mm_jobs(jobs, 512, [up, wdT])
            ji = 0
            for q in range(2):
                for oh in range(2):
                    pb = jobs[ji][0]; ji += 1
                    B.op("vector", lambda e, q=q, oh=oh, pb=pb: e.scalar_tensor_tensor(
                        y2p.ap[:, q, oh * 512:(oh + 1) * 512], pb.ap, S127 * S127,
                        y1tc.ap[:, q, oh * 512:(oh + 1) * 512], ALU.mult, ALU.add),
                        reads=[pb, y1tc], writes=[y2p])
            ln_stats(y2p, 2, stat)
            def napplyD(e, stat=stat):
                for q in range(2):
                    e.activation(y1tc.ap[:, q, :], y2p.ap[:, q, :], AF.Identity,
                                 bias=stat.ap[:, 24 + q:25 + q],
                                 scale=stat.ap[:, 20 + q:21 + q])
                return e.nop()
            B.op("scalar", napplyD, reads=[y2p, stat], writes=[y1tc])
            B.op("sync", lambda e, t0=t0: e.dma_start(
                out_h[t0:t0 + 256, :].rearrange("(q p) d -> p q d", p=128), y1tc.ap),
                reads=[y1tc], dma=True)
    ar.release()

    B.raw("sync", lambda e: e.wait_ge(B.dsems["sync"].h, B.dsems["sync"].n))
    B.raw("sync", lambda e: e.wait_ge(B.dsems["scalar"].h, B.dsems["scalar"].n))
    B.raw("gpsimd", lambda e: e.wait_ge(B.dsems["gpsimd"].h, B.dsems["gpsimd"].n))

    with nc.Block() as blk:
        @blk.sync
        def _(e):
            for fn in B.ops["sync"]:
                fn(e)

        @blk.gpsimd
        def _(e):
            for fn in B.ops["gpsimd"]:
                fn(e)

        @blk.tensor
        def _(e):
            for fn in B.ops["tensor"]:
                fn(e)

        @blk.vector
        def _(e):
            for fn in B.ops["vector"]:
                fn(e)

        @blk.scalar
        def _(e):
            for fn in B.ops["scalar"]:
                fn(e)

    stack.close()
    return nc


_IN_MAP = {
    "wk": "L0_attn_w_key", "wv": "L0_attn_w_value", "wr": "L0_attn_w_recept",
    "wo": "L0_attn_w_output", "fwk": "L0_attn_f_w_key", "fwr": "L0_attn_f_w_recept",
    "fwv": "L0_attn_f_w_value", "wg": "L0_ffn_w_gate", "wu": "L0_ffn_w_up",
    "wd": "L0_ffn_w_down", "ln1w": "L0_attn_ln1_w", "ln1b": "L0_attn_ln1_b",
    "ln2w": "L0_attn_ln2_w", "ln2b": "L0_attn_ln2_b", "tdec": "L0_attn_time_decay",
    "tfir": "L0_attn_time_first", "tmk": "L0_attn_time_mix_k",
    "tmv": "L0_attn_time_mix_v", "tmr": "L0_attn_time_mix_r",
    "ftmk": "L0_attn_f_time_mix_k", "ftmr": "L0_attn_f_time_mix_r",
}


def kernel(**inputs):
    nc = build(DEBUG_PHASES, DEBUG_OUTS)
    shared = {}
    for k, src in _IN_MAP.items():
        a = np.ascontiguousarray(np.asarray(inputs[src]))
        if a.ndim == 3 and a.shape[0] == 1:
            a = a.reshape(-1)
        shared[k] = a
    x_full = np.ascontiguousarray(np.asarray(inputs["L0_attn_x"]))
    in_maps = []
    for b in range(8):
        m = dict(shared)
        m["x"] = x_full[b]
        in_maps.append(m)
    res = run_bass_kernel_spmd(nc, in_maps, core_ids=list(range(8)))
    return np.stack([r["out"] for r in res.results], axis=0)


# version 16
# speedup vs baseline: 1.0033x; 1.0033x over previous
import numpy as np

import concourse.bass as bass
import concourse.mybir as mybir
from concourse.bass_utils import run_bass_kernel_spmd

F32 = mybir.dt.float32
BF16 = mybir.dt.bfloat16
I32 = mybir.dt.int32
AF = mybir.ActivationFunctionType
ALU = mybir.AluOpType
AX = mybir.AxisListType

T, D, FH, HH = 4096, 1024, 4096, 3072
DC = D // 128
FC = FH // 128
HC = HH // 128
S127 = 1.0 / 127.0
EPS = 1e-5

DEBUG_PHASES = 99
DEBUG_OUTS = ()

ENGS = ("sync", "gpsimd", "tensor", "vector", "scalar")


class Sem:
    def __init__(self, name, eng=None):
        self.name, self.n, self.h, self.eng = name, 0, None, eng


class Buf:

    def __init__(self, ap, name=""):
        self.ap = ap
        self.name = name
        self.w = None
        self.readers = []


class Builder:
    def __init__(self, nc):
        self.nc = nc
        self.ops = {e: [] for e in ENGS}
        self.sems = {e: Sem("sem_" + e, e) for e in ENGS}
        self.dsems = {q: [Sem(f"d_{q}{k}") for k in range(8)]
                      for q in ("sync", "gpsimd", "scalar")}
        self.didx = {q: 0 for q in ("sync", "gpsimd", "scalar")}
        self.waited = {e: {} for e in ENGS}
        self.force_drain = True

    def all_sems(self):
        out = list(self.sems.values())
        for q in self.dsems.values():
            out += q
        return out

    def _wait(self, eng, tok):
        sem, val, src, flag = tok
        if src == eng:
            return
        prev = self.waited[eng].get(sem.name, -1)
        if val <= prev:
            return
        self.waited[eng][sem.name] = val
        flag["x"] = True
        self.ops[eng].append(lambda e, s=sem, v=val: e.wait_ge(s.h, v))

    def op(self, eng, build, reads=(), writes=(), dma=False):
        if dma:
            for b in list(reads) + list(writes):
                if b.w is not None and b.w[2] == eng:
                    b.w[3]["x"] = True
                for t in b.readers:
                    if t[2] == eng:
                        t[3]["x"] = True
        for b in reads:
            if b.w is not None:
                self._wait(eng, b.w)
        for b in writes:
            if b.w is not None:
                self._wait(eng, b.w)
            for t in b.readers:
                self._wait(eng, t)
        if dma:
            sem = self.dsems[eng][self.didx[eng] % 8]
            self.didx[eng] += 1
            if sem.n > 0:
                self._wait(eng, (sem, sem.n, None, {"x": True}))
            sem.n += 16
            step = 16
        else:
            sem = self.sems[eng]
            sem.n += 1
            step = 1
        val = sem.n
        flag = {"x": self.force_drain}
        tok = (sem, val, None if dma else eng, flag)

        if dma or eng == "sync":
            def fn(e, build=build, sem=sem, step=step):
                ins = build(e)
                ins.then_inc(sem.h, step)
        else:
            def fn(e, build=build, sem=sem, step=step, flag=flag):
                ins = build(e)
                if flag["x"]:
                    e.drain().then_inc(sem.h, step)
                else:
                    ins.then_inc(sem.h, step)

        self.ops[eng].append(fn)
        for b in reads:
            b.readers.append(tok)
        for b in writes:
            b.w = tok
            b.readers = []
        return tok

    def raw(self, eng, fn):
        self.ops[eng].append(fn)

    def barrier(self):
        toks = []
        for e in ("gpsimd", "tensor", "vector", "scalar"):
            sem = self.sems[e]
            sem.n += 1
            self.ops[e].append(lambda eng, s=sem: eng.drain().then_inc(s.h, 1))
            toks.append((sem, sem.n, e, {"x": True}))
        for q in self.dsems.values():
            for ds in q:
                toks.append((ds, ds.n, None, {"x": True}))
        for e in ENGS:
            for t in toks:
                self._wait(e, t)


class Arena:

    def __init__(self, sb, cols):
        self.sb, self.cols, self.top, self.marks = sb, cols, 0, []

    def mark(self):
        self.marks.append(self.top)

    def release(self):
        self.top = self.marks.pop()

    def alloc(self, ncols, name=""):
        a = (self.top + 7) & ~7
        self.top = a + ncols
        assert self.top <= self.cols, f"arena overflow at {name}: {self.top}>{self.cols}"
        return a

    def f32(self, shape, name=""):
        n = int(np.prod(shape[1:]))
        a = self.alloc(n, name)
        ap = self.sb[:, a:a + n]
        if len(shape) > 2:
            ap = ap.rearrange("p (a b) -> p a b", a=shape[1])
        return Buf(ap, name)

    def bf16(self, shape, name=""):
        n = int(np.prod(shape[1:]))
        a = self.alloc((n + 1) // 2, name)
        ap = self.sb[:, a:a + (n + 1) // 2].bitcast(BF16)
        if len(shape) > 2:
            ap = ap.rearrange("p (a b) -> p a b", a=shape[1])
        return Buf(ap, name)


class PsumPool:
    def __init__(self, ps):
        self.banks = [Buf(ps[:, b, :], f"psum{b}") for b in range(8)]
        self.i = 0

    def take(self):
        b = self.banks[self.i]
        self.i = (self.i + 1) % 8
        return b


def build(debug_phases=99, debug_outs=()):
    nc = bass.Bass("TRN2", target_bir_lowering=False)

    def dram(name, shape, dtype, kind):
        if name in debug_outs:
            kind = "ExternalOutput"
        return nc.dram_tensor(name, shape, dtype, kind=kind).ap()

    x_h = dram("x", [T, D], F32, "ExternalInput")
    wk_h = dram("wk", [D, D], F32, "ExternalInput")
    wv_h = dram("wv", [D, D], F32, "ExternalInput")
    wr_h = dram("wr", [D, D], F32, "ExternalInput")
    wo_h = dram("wo", [D, D], F32, "ExternalInput")
    fwk_h = dram("fwk", [D, FH], F32, "ExternalInput")
    fwr_h = dram("fwr", [D, D], F32, "ExternalInput")
    fwv_h = dram("fwv", [FH, D], F32, "ExternalInput")
    wg_h = dram("wg", [HH, D], I32, "ExternalInput")
    wu_h = dram("wu", [HH, D], I32, "ExternalInput")
    wd_h = dram("wd", [D, HH], I32, "ExternalInput")
    pv_h = {}
    for p in ("ln1w", "ln1b", "ln2w", "ln2b", "tdec", "tfir",
              "tmk", "tmv", "tmr", "ftmk", "ftmr"):
        pv_h[p] = dram(p, [D], F32, "ExternalInput")
    out_h = dram("out", [T, D], F32, "ExternalOutput")
    dbg_h = dram("dbg", [128, 16384], F32, "Internal")

    k_fm = dram("k_fm", [D, T], F32, "Internal")
    v_fm = dram("v_fm", [D, T], F32, "Internal")
    r_fm = dram("r_fm", [D, T], F32, "Internal")
    y_fm = dram("y_fm", [D, T], BF16, "Internal")
    s_tm = dram("s_tm", [T, D], F32, "Internal")
    fk2_fm = dram("fk2_fm", [FH, T], BF16, "Internal")
    sfr_tm = dram("sfr_tm", [T, D], BF16, "Internal")
    y1_tm = dram("y1_tm", [T, D], F32, "Internal")
    y1_fm = dram("y1_fm", [D, T], BF16, "Internal")
    wg_bf = dram("wg_bf", [HH, D], BF16, "Internal")
    wu_bf = dram("wu_bf", [HH, D], BF16, "Internal")
    wd_bf = dram("wd_bf", [D, HH], BF16, "Internal")
    h_fm = dram("h_fm", [HH, T], BF16, "Internal")
    HB = {n: Buf(None, "HB_" + n) for n in
          ("k_fm", "v_fm", "r_fm", "y_fm", "s_tm", "fk2_fm", "sfr_tm",
           "y1_tm", "y1_fm", "wg_bf", "wu_bf", "wd_bf", "h_fm")}

    ARENA_COLS = 52224
    import contextlib
    stack = contextlib.ExitStack()
    sb = stack.enter_context(nc.sbuf_tensor("arena", [128, ARENA_COLS], F32))
    ps = stack.enter_context(nc.psum_tensor("psum", [128, 8, 512], F32))

    B = Builder(nc)
    for s in B.all_sems():
        s.h = stack.enter_context(nc.semaphore(s.name))

    ar = Arena(sb, ARENA_COLS)
    pool = PsumPool(ps)

    ident = ar.f32([128, 128], "ident")
    P = {p: ar.f32([128, DC], p) for p in pv_h}
    dw = ar.f32([128, DC], "dw")
    eu = ar.f32([128, DC], "eu")
    mhalf = ar.f32([128, 8], "mhalf")
    epsb = ar.f32([128, 1], "epsb")
    stat2 = [ar.f32([128, 32], f"stat{j}") for j in range(2)]
    sq = ar.f32([128, 1024], "sq")

    B.op("gpsimd", lambda e: e.memset(ident.ap, 0.0), writes=[ident])
    B.op("gpsimd", lambda e: e.affine_select(
        out=ident.ap, in_=ident.ap, compare_op=ALU.not_equal, fill=1.0,
        base=0, pattern=[[-1, 128]], channel_multiplier=1),
        reads=[ident], writes=[ident])
    B.op("gpsimd", lambda e: e.memset(mhalf.ap, -0.5), writes=[mhalf])
    B.op("gpsimd", lambda e: e.memset(epsb.ap, EPS), writes=[epsb])
    for p, h in pv_h.items():
        B.op("gpsimd", lambda e, h=h, p=p: e.dma_start(
            P[p].ap, h.rearrange("(c p) -> p c", p=128),
            allow_slow_non_contiguous=True), writes=[P[p]], dma=True)
    B.op("scalar", lambda e: e.activation(dw.ap, P["tdec"].ap, AF.Exp),
         reads=[P["tdec"]], writes=[dw])
    B.op("scalar", lambda e: e.activation(dw.ap, dw.ap, AF.Exp, scale=-1.0),
         reads=[dw], writes=[dw])
    B.op("scalar", lambda e: e.activation(eu.ap, P["tfir"].ap, AF.Exp),
         reads=[P["tfir"]], writes=[eu])

    _dq_jobs = []
    for (src, dsth, dstbn, nch) in ((wg_h, wg_bf, "wg_bf", HC),
                                    (wu_h, wu_bf, "wu_bf", HC),
                                    (wd_h, wd_bf, "wd_bf", DC)):
        for c in range(nch):
            _dq_jobs.append((src, dsth, dstbn, c))

    def emit_ffn_dequant(frac_lo, frac_hi):
        lo = int(len(_dq_jobs) * frac_lo)
        hi = int(len(_dq_jobs) * frac_hi)
        for idx in range(lo, hi):
            src, dsth, dstbn, c = _dq_jobs[idx]
            width = src.shape[1]
            stgb = wstg[idx % 2]
            B.op("gpsimd", lambda e, src=src, c=c, stgb=stgb, width=width:
                 e.dma_start(stgb.ap[:, 0:width], src[c * 128:(c + 1) * 128, :]),
                 writes=[stgb], dma=True)
            B.op("gpsimd", lambda e, dsth=dsth, c=c, stgb=stgb, width=width:
                 e.dma_start(dsth[c * 128:(c + 1) * 128, :], stgb.ap[:, 0:width]),
                 reads=[stgb], writes=[HB[dstbn]], dma=True)

    def ln_stats(xbuf, nsub, stat):
        xa = xbuf.ap

        def reds(e, nsub=nsub, xa=xa, stat=stat):
            for q in range(nsub):
                e.reduce_sum(stat.ap[:, q:q + 1], xa[:, q, :], AX.X)
            return e.nop()
        B.op("vector", reds, reads=[xbuf], writes=[stat])

        def sqs(e, nsub=nsub, xa=xa, stat=stat):
            for q in range(nsub):
                e.activation(sq.ap, xa[:, q, :], AF.Square,
                             accum_out=stat.ap[:, 4 + q:5 + q])
            return e.nop()
        B.op("scalar", sqs, reads=[xbuf], writes=[stat, sq])
        B.op("scalar", lambda e: e.activation(
            stat.ap[:, 8:8 + nsub], stat.ap[:, 0:nsub], AF.Identity, scale=1.0 / D),
            reads=[stat], writes=[stat])
        B.op("scalar", lambda e: e.activation(
            stat.ap[:, 12:12 + nsub], stat.ap[:, 8:8 + nsub], AF.Square),
            reads=[stat], writes=[stat])
        B.op("scalar", lambda e: e.activation(
            stat.ap[:, 28:28 + nsub], stat.ap[:, 4:4 + nsub], AF.Identity,
            scale=1.0 / D, bias=epsb.ap),
            reads=[stat, epsb], writes=[stat])
        B.op("gpsimd", lambda e: e.tensor_tensor(
            stat.ap[:, 16:16 + nsub], stat.ap[:, 28:28 + nsub],
            stat.ap[:, 12:12 + nsub], ALU.subtract),
            reads=[stat], writes=[stat])
        B.op("gpsimd", lambda e: e.tensor_scalar(
            stat.ap[:, 16:16 + nsub], stat.ap[:, 16:16 + nsub], 1e-6, None, ALU.max),
            reads=[stat], writes=[stat])
        B.op("gpsimd", lambda e: e.tensor_tensor(
            stat.ap[:, 20:20 + nsub], stat.ap[:, 16:16 + nsub],
            mhalf.ap[:, 0:nsub], ALU.pow),
            reads=[stat, mhalf], writes=[stat])
        B.op("scalar", lambda e: e.activation(
            stat.ap[:, 12:12 + nsub], stat.ap[:, 8:8 + nsub], AF.Identity, scale=-1.0),
            reads=[stat], writes=[stat])
        B.op("gpsimd", lambda e: e.tensor_tensor(
            stat.ap[:, 24:24 + nsub], stat.ap[:, 12:12 + nsub],
            stat.ap[:, 20:20 + nsub], ALU.mult),
            reads=[stat, mhalf], writes=[stat])

    def mm_tile(psumb, n, lhsT_buf, lhsT_aps, rhs_buf, rhs_aps):
        mm_jobs([(psumb, lhsT_aps, rhs_aps)], n, [lhsT_buf, rhs_buf])

    def mm_jobs(jobs, n, reads):

        def grp(e):
            last = None
            for psumb, lts, rhs in jobs:
                cnt = len(lts)
                for i in range(cnt):
                    last = e.matmul(psumb.ap[:, 0:n], lts[i], rhs[i],
                                    start=(i == 0), stop=(i == cnt - 1))
            return last
        B.op("tensor", grp, reads=reads, writes=[j[0] for j in jobs])

    def transpose_group(dcs_qs, src_buf, src_aps, bank):

        def grp(e):
            last = None
            for idx, ap in enumerate(src_aps):
                last = e.transpose(bank.ap[:, idx * 128:(idx + 1) * 128], ap, ident.ap)
            return last
        B.op("tensor", grp, reads=[src_buf, ident], writes=[bank])

    ar.mark()
    wstg = [ar.bf16([128, 3072], "wstg0")] * 2
    wk_sb = ar.bf16([128, DC, 1024], "wk_sb")
    wv_sb = ar.bf16([128, DC, 1024], "wv_sb")
    wr_sb = ar.bf16([128, DC, 1024], "wr_sb")
    x_t2 = [ar.f32([128, 4, 1024], f"x_t{j}") for j in range(2)]
    n_t = ar.f32([128, 4, 1024], "n_t")
    xn = ar.f32([128, DC, 513], "xn")
    dnc = ar.f32([128, 512], "dnc")
    m_k2 = [ar.bf16([128, DC, 512], f"m_k{j}") for j in range(2)]
    m_v2 = [ar.bf16([128, DC, 512], f"m_v{j}") for j in range(2)]
    m_r2 = [ar.bf16([128, DC, 512], f"m_r{j}") for j in range(2)]
    stg = {p: [ar.f32([128, 512], f"stg_{p}{j}") for j in range(4)]
           for p in ("k", "v", "r")}

    for w_h, w_sb in ((wk_h, wk_sb), (wv_h, wv_sb), (wr_h, wr_sb)):
        B.op("gpsimd", lambda e, w_h=w_h, w_sb=w_sb: e.dma_start(
            w_sb.ap, w_h.rearrange("(c p) o -> p c o", p=128)),
            writes=[w_sb], dma=True)
    B.op("gpsimd", lambda e: e.memset(xn.ap[:, :, 0:1], 0.0), writes=[xn])

    def emit_projA(i):
        t0 = 512 * i
        m_k, m_v, m_r = m_k2[i % 2], m_v2[i % 2], m_r2[i % 2]
        for name, w_sb, m, dst, hb in (("k", wk_sb, m_k, k_fm, HB["k_fm"]),
                                       ("v", wv_sb, m_v, v_fm, HB["v_fm"]),
                                       ("r", wr_sb, m_r, r_fm, HB["r_fm"])):
            for ob in range(0, DC, 4):
                jobs = []
                for oc in range(ob, ob + 4):
                    pb = pool.take()
                    jobs.append((pb, [w_sb.ap[:, kc, oc * 128:(oc + 1) * 128] for kc in range(DC)],
                                 [m.ap[:, kc, :] for kc in range(DC)]))
                mm_jobs(jobs, 512, [w_sb, m])
                sgs = [stg[name][oc % 4] for oc in range(ob, ob + 4)]

                def evac4(e, jobs=jobs, sgs=sgs):
                    for k4 in range(4):
                        e.activation(sgs[k4].ap, jobs[k4][0].ap, AF.Copy)
                    return e.nop()
                B.op("scalar", evac4, reads=[j[0] for j in jobs], writes=sgs)
                for k4, oc in enumerate(range(ob, ob + 4)):
                    B.op("scalar", lambda e, dst=dst, oc=oc, t0=t0, sg_=sgs[k4]: e.dma_start(
                        dst[oc * 128:(oc + 1) * 128, t0:t0 + 512], sg_.ap),
                        reads=[sgs[k4]], writes=[hb], dma=True)

    if debug_phases >= 1:
        for i in range(8):
            t0 = 512 * i
            x_t = x_t2[i % 2]
            m_k, m_v, m_r = m_k2[i % 2], m_v2[i % 2], m_r2[i % 2]
            stat = stat2[i % 2]
            B.op("sync", lambda e, t0=t0, x_t=x_t: e.dma_start(
                x_t.ap, x_h[t0:t0 + 512, :].rearrange("(q p) d -> p q d", p=128)),
                writes=[x_t], dma=True)
            ln_stats(x_t, 4, stat)
            def napplyA(e, n_t=n_t, x_t=x_t, stat=stat):
                for q in range(4):
                    e.activation(n_t.ap[:, q, :], x_t.ap[:, q, :], AF.Identity,
                                 bias=stat.ap[:, 24 + q:25 + q],
                                 scale=stat.ap[:, 20 + q:21 + q])
                return e.nop()
            B.op("scalar", napplyA, reads=[x_t, stat], writes=[n_t])
            for dc in range(DC):
                pb = pool.take()
                transpose_group(None, n_t,
                                [n_t.ap[:, q, dc * 128:(dc + 1) * 128] for q in range(4)], pb)
                B.op("scalar", lambda e, dc=dc, pb=pb: e.activation(
                    xn.ap[:, dc, 1:513], pb.ap[:, 0:512],
                    AF.Identity, bias=P["ln1b"].ap[:, dc:dc + 1],
                    scale=P["ln1w"].ap[:, dc:dc + 1]),
                    reads=[pb, P["ln1w"], P["ln1b"]], writes=[xn])
            for dc in range(DC):
                B.op("vector", lambda e, dc=dc: e.scalar_tensor_tensor(
                    dnc.ap, xn.ap[:, dc, 1:513], 1.0, xn.ap[:, dc, 0:512],
                    ALU.mult, ALU.subtract),
                    reads=[xn], writes=[dnc])

                def mixes(e, dc=dc, m_k=m_k, m_v=m_v, m_r=m_r):
                    for tm, m in (("tmk", m_k), ("tmv", m_v), ("tmr", m_r)):
                        e.scalar_tensor_tensor(
                            m.ap[:, dc, :], dnc.ap, P[tm].ap[:, dc:dc + 1],
                            xn.ap[:, dc, 0:512], ALU.mult, ALU.add)
                    return e.nop()
                B.op("vector", mixes,
                     reads=[dnc, xn, P["tmk"], P["tmv"], P["tmr"]],
                     writes=[m_k, m_v, m_r])
            B.op("gpsimd", lambda e: e.tensor_copy(xn.ap[:, :, 0:1], xn.ap[:, :, 512:513]),
                 reads=[xn], writes=[xn])
            if i >= 1:
                emit_projA(i - 1)
            emit_ffn_dequant(i / 8.0, (i + 1) / 8.0)
        emit_projA(7)
    ar.release()
    B.barrier()

    ar.mark()
    kb2 = [ar.f32([128, 4096], f"kb{j}") for j in range(2)]
    vb2 = [ar.f32([128, 4096], f"vb{j}") for j in range(2)]
    rb2 = [ar.f32([128, 4096], f"rb{j}") for j in range(2)]
    ek = ar.f32([128, 4096], "ek")
    ekv = ar.f32([128, 4096], "ekv")
    tmp = ar.f32([128, 4096], "tmp")
    Abuf = ar.f32([128, 4100], "Abuf")
    Bbuf = ar.f32([128, 4100], "Bbuf")
    yb2 = [ar.bf16([128, 4096], f"yb{j}") for j in range(2)]
    kmax2 = [ar.f32([128, 2], f"kmax{j}") for j in range(2)]

    B.force_drain = True
    B.op("gpsimd", lambda e: e.memset(Abuf.ap[:, 0:1], 0.0), writes=[Abuf])
    B.op("gpsimd", lambda e: e.memset(Bbuf.ap[:, 0:1], 0.0), writes=[Bbuf])

    if debug_phases >= 2:
        for j in range(DC):
            kb, vb, rb = kb2[j % 2], vb2[j % 2], rb2[j % 2]
            yb, kmax = yb2[j % 2], kmax2[j % 2]
            for src, hb, b in ((k_fm, HB["k_fm"], kb), (v_fm, HB["v_fm"], vb),
                               (r_fm, HB["r_fm"], rb)):
                B.op("sync", lambda e, src=src, j=j, b=b: e.dma_start(
                    b.ap, src[j * 128:(j + 1) * 128, :]),
                    reads=[hb], writes=[b], dma=True)
            if j == 7 and "dbg" in debug_outs:
                B.op("sync", lambda e: e.dma_start(dbg_h[:, 0:4096], rb.ap),
                     reads=[rb], dma=True)
            B.op("scalar", lambda e, rb=rb: e.activation(
                tmp.ap, rb.ap, AF.Exp, scale=-1.0), reads=[rb], writes=[tmp])
            B.op("vector", lambda e, kmax=kmax, kb=kb: e.reduce_max(
                kmax.ap[:, 0:1], kb.ap[:, 0:512], AX.X), reads=[kb], writes=[kmax])
            B.op("vector", lambda e, kmax=kmax: e.tensor_scalar(
                kmax.ap[:, 1:2], kmax.ap[:, 0:1], -1.0, -20.0, ALU.mult, ALU.max),
                reads=[kmax], writes=[kmax])
            B.op("vector", lambda e, kmax=kmax: e.tensor_scalar(
                kmax.ap[:, 1:2], kmax.ap[:, 1:2], 20.0, None, ALU.min),
                reads=[kmax], writes=[kmax])
            B.op("scalar", lambda e, kb=kb, kmax=kmax: e.activation(
                ek.ap, kb.ap, AF.Exp, bias=kmax.ap[:, 1:2]),
                reads=[kb, kmax], writes=[ek])
            B.op("vector", lambda e, vb=vb: e.scalar_tensor_tensor(
                ekv.ap, ek.ap, 1.0, vb.ap, ALU.mult, ALU.mult),
                reads=[ek, vb], writes=[ekv])
            B.op("vector", lambda e, j=j: e.tensor_tensor_scan(
                Abuf.ap[:, 1:4097], dw.ap[:, j:j + 1].to_broadcast([128, 4096]),
                ekv.ap, 0.0, ALU.mult, ALU.add),
                reads=[ekv, dw], writes=[Abuf])
            B.op("vector", lambda e, j=j: e.tensor_tensor_scan(
                Bbuf.ap[:, 1:4097], dw.ap[:, j:j + 1].to_broadcast([128, 4096]),
                ek.ap, 0.0, ALU.mult, ALU.add),
                reads=[ek, dw], writes=[Bbuf])
            B.op("vector", lambda e, j=j, kb=kb: e.scalar_tensor_tensor(
                kb.ap, ekv.ap, eu.ap[:, j:j + 1], Abuf.ap[:, 0:4096],
                ALU.mult, ALU.add),
                reads=[ekv, eu, Abuf], writes=[kb])
            B.op("vector", lambda e, j=j, vb=vb: e.scalar_tensor_tensor(
                vb.ap, ek.ap, eu.ap[:, j:j + 1], Bbuf.ap[:, 0:4096],
                ALU.mult, ALU.add),
                reads=[ek, eu, Bbuf], writes=[vb])
            B.op("vector", lambda e, vb=vb: e.scalar_tensor_tensor(
                ekv.ap, tmp.ap, 1.0, vb.ap, ALU.add, ALU.mult),
                reads=[tmp, vb], writes=[ekv])
            B.op("scalar", lambda e: e.activation(
                tmp.ap, ekv.ap, AF.Ln), reads=[ekv], writes=[tmp])
            B.op("scalar", lambda e: e.activation(
                tmp.ap, tmp.ap, AF.Exp, scale=-1.0), reads=[tmp], writes=[tmp])
            if j == 7 and "dbg" in debug_outs:
                B.op("sync", lambda e: e.dma_start(dbg_h[:, 4096:8192], Abuf.ap[:, 0:4096]),
                     reads=[Abuf], dma=True)
                B.op("sync", lambda e: e.dma_start(dbg_h[:, 8192:12288], kb.ap),
                     reads=[kb], dma=True)
                B.op("sync", lambda e: e.dma_start(dbg_h[:, 12288:12290], kmax.ap),
                     reads=[kmax], dma=True)
                B.op("sync", lambda e: e.dma_start(
                    dbg_h[:, 12296:14344].bitcast(BF16), yb.ap), reads=[yb], dma=True)

            B.op("vector", lambda e, yb=yb, kb=kb: e.scalar_tensor_tensor(
                yb.ap, kb.ap, 1.0, tmp.ap, ALU.mult, ALU.mult),
                reads=[kb, tmp], writes=[yb])
            B.op("scalar", lambda e, j=j, yb=yb: e.dma_start(
                y_fm[j * 128:(j + 1) * 128, :], yb.ap),
                reads=[yb], writes=[HB["y_fm"]], dma=True)
    B.force_drain = False
    ar.release()
    B.barrier()

    ar.mark()
    wo_sb = ar.bf16([128, DC, 1024], "wo_sb")
    fwk_sb = ar.bf16([128, DC, 4096], "fwk_sb")
    fwr_sb = ar.bf16([128, DC, 1024], "fwr_sb")
    xc2 = [ar.f32([128, 2, 1024], f"xc{j}") for j in range(2)]
    yc2 = [ar.bf16([128, DC, 256], f"yc{j}") for j in range(2)]
    x1 = ar.f32([128, 2, 1024], "x1")
    sc = ar.f32([128, 2, 1024], "sc")
    n2 = ar.f32([128, 2, 1024], "n2")
    xn2 = ar.f32([128, DC, 257], "xn2")
    dn2 = ar.f32([128, 256], "dn2")
    m2k2 = [ar.bf16([128, DC, 256], f"m2k{j}") for j in range(2)]
    m2r2 = [ar.bf16([128, DC, 256], f"m2r{j}") for j in range(2)]
    fkst = ar.bf16([128, FC, 256], "fkst")
    relu_t = ar.f32([128, 1024], "relu_t")
    sfrst = ar.bf16([128, 2, 1024], "sfrst")

    if debug_phases >= 3:
        for w_h, w_sb in ((wo_h, wo_sb), (fwk_h, fwk_sb), (fwr_h, fwr_sb)):
            B.op("gpsimd", lambda e, w_h=w_h, w_sb=w_sb: e.dma_start(
                w_sb.ap, w_h.rearrange("(c p) o -> p c o", p=128)),
                writes=[w_sb], dma=True)
        B.op("gpsimd", lambda e: e.memset(xn2.ap[:, :, 0:1], 0.0), writes=[xn2])

        def emit_fkfr(i):
            t0 = 256 * i
            m2k, m2r = m2k2[i % 2], m2r2[i % 2]
            for fb in range(0, FC, 4):
                jobs = []
                for fc in range(fb, fb + 4):
                    pb = pool.take()
                    jobs.append((pb, [fwk_sb.ap[:, kc, fc * 128:(fc + 1) * 128] for kc in range(DC)],
                                 [m2k.ap[:, kc, :] for kc in range(DC)]))
                mm_jobs(jobs, 256, [fwk_sb, m2k])

                def relu4(e, jobs=jobs):
                    for k4 in range(4):
                        e.tensor_scalar(relu_t.ap[:, k4 * 256:(k4 + 1) * 256],
                                        jobs[k4][0].ap[:, 0:256], 0.0, None, ALU.max)
                    return e.nop()
                B.op("vector", relu4, reads=[j[0] for j in jobs], writes=[relu_t])
                B.op("scalar", lambda e, fb=fb, fkst=fkst: e.activation(
                    fkst.ap[:, fb:fb + 4, :].rearrange("p a b -> p (a b)"),
                    relu_t.ap, AF.Square),
                    reads=[relu_t], writes=[fkst])
            B.op("scalar", lambda e, t0=t0, fkst=fkst: e.dma_start(
                fk2_fm.rearrange("(c p) t -> p c t", p=128)[:, :, t0:t0 + 256], fkst.ap),
                reads=[fkst], writes=[HB["fk2_fm"]], dma=True)
            for q in range(2):
                for oh in range(2):
                    pb = pool.take()
                    mm_tile(pb, 512, m2r,
                            [m2r.ap[:, kc, q * 128:(q + 1) * 128] for kc in range(DC)],
                            fwr_sb,
                            [fwr_sb.ap[:, kc, oh * 512:(oh + 1) * 512] for kc in range(DC)])
                    B.op("scalar", lambda e, q=q, oh=oh, pb=pb: e.activation(
                        sfrst.ap[:, q, oh * 512:(oh + 1) * 512], pb.ap, AF.Sigmoid),
                        reads=[pb], writes=[sfrst])
            B.op("scalar", lambda e, t0=t0: e.dma_start(
                sfr_tm[t0:t0 + 256, :].rearrange("(q p) d -> p q d", p=128), sfrst.ap),
                reads=[sfrst], writes=[HB["sfr_tm"]], dma=True)

        for i in range(16):
            t0 = 256 * i
            xc, yc = xc2[i % 2], yc2[i % 2]
            m2k, m2r = m2k2[i % 2], m2r2[i % 2]
            stat = stat2[i % 2]
            B.op("sync", lambda e, t0=t0, xc=xc: e.dma_start(
                xc.ap, x_h[t0:t0 + 256, :].rearrange("(q p) d -> p q d", p=128)),
                writes=[xc], dma=True)
            B.op("sync", lambda e, t0=t0, yc=yc: e.dma_start(
                yc.ap, y_fm.rearrange("(c p) t -> p c t", p=128)[:, :, t0:t0 + 256]),
                reads=[HB["y_fm"]], writes=[yc], dma=True)
            jobs = []
            for q in range(2):
                for oh in range(2):
                    pb = pool.take()
                    jobs.append((pb, [yc.ap[:, kc, q * 128:(q + 1) * 128] for kc in range(DC)],
                                 [wo_sb.ap[:, kc, oh * 512:(oh + 1) * 512] for kc in range(DC)]))
            mm_jobs(jobs, 512, [yc, wo_sb])
            ji = 0
            for q in range(2):
                for oh in range(2):
                    pb = jobs[ji][0]; ji += 1
                    B.op("vector", lambda e, q=q, oh=oh, pb=pb, xc=xc: e.scalar_tensor_tensor(
                        x1.ap[:, q, oh * 512:(oh + 1) * 512], pb.ap, 1.0,
                        xc.ap[:, q, oh * 512:(oh + 1) * 512], ALU.mult, ALU.add),
                        reads=[pb, xc], writes=[x1])
            for q in range(2):
                B.op("vector", lambda e, q=q, xc=xc: e.scalar_tensor_tensor(
                    sc.ap[:, q, :], x1.ap[:, q, :], 1.0, xc.ap[:, q, :],
                    ALU.mult, ALU.add), reads=[x1, xc], writes=[sc])
            B.op("scalar", lambda e, t0=t0: e.dma_start(
                s_tm[t0:t0 + 256, :].rearrange("(q p) d -> p q d", p=128), sc.ap),
                reads=[sc], writes=[HB["s_tm"]], dma=True)
            ln_stats(x1, 2, stat)
            def napplyC(e, stat=stat):
                for q in range(2):
                    e.activation(n2.ap[:, q, :], x1.ap[:, q, :], AF.Identity,
                                 bias=stat.ap[:, 24 + q:25 + q],
                                 scale=stat.ap[:, 20 + q:21 + q])
                return e.nop()
            B.op("scalar", napplyC, reads=[x1, stat], writes=[n2])
            if i >= 1:
                emit_fkfr(i - 1)
            for dc in range(DC):
                pb = pool.take()
                transpose_group(None, n2,
                                [n2.ap[:, q, dc * 128:(dc + 1) * 128] for q in range(2)], pb)
                B.op("scalar", lambda e, dc=dc, pb=pb: e.activation(
                    xn2.ap[:, dc, 1:257], pb.ap[:, 0:256],
                    AF.Identity, bias=P["ln2b"].ap[:, dc:dc + 1],
                    scale=P["ln2w"].ap[:, dc:dc + 1]),
                    reads=[pb, P["ln2w"], P["ln2b"]], writes=[xn2])
            for dc in range(DC):
                B.op("vector", lambda e, dc=dc: e.scalar_tensor_tensor(
                    dn2.ap, xn2.ap[:, dc, 1:257], 1.0, xn2.ap[:, dc, 0:256],
                    ALU.mult, ALU.subtract), reads=[xn2], writes=[dn2])

                def mixes2(e, dc=dc, m2k=m2k, m2r=m2r):
                    for tm, m in (("ftmk", m2k), ("ftmr", m2r)):
                        e.scalar_tensor_tensor(
                            m.ap[:, dc, :], dn2.ap, P[tm].ap[:, dc:dc + 1],
                            xn2.ap[:, dc, 0:256], ALU.mult, ALU.add)
                    return e.nop()
                B.op("vector", mixes2, reads=[dn2, xn2, P["ftmk"], P["ftmr"]],
                     writes=[m2k, m2r])
            B.op("gpsimd", lambda e: e.tensor_copy(xn2.ap[:, :, 0:1], xn2.ap[:, :, 256:257]),
                 reads=[xn2], writes=[xn2])
        emit_fkfr(15)
    ar.release()
    B.barrier()

    ar.mark()
    fwv_sb = ar.bf16([128, FC, 1024], "fwv_sb")
    fk2c2 = [ar.bf16([128, FC, 512], f"fk2c{j}") for j in range(2)]
    sfrc = ar.bf16([128, 4, 1024], "sfrc")
    scc = ar.f32([128, 4, 1024], "scc")
    y1p = ar.f32([128, 4, 1024], "y1p")
    y1t = ar.f32([128, 4, 1024], "y1t")
    y1f = ar.bf16([128, DC, 512], "y1f")

    if debug_phases >= 4:
        B.op("gpsimd", lambda e: e.dma_start(
            fwv_sb.ap, fwv_h.rearrange("(c p) o -> p c o", p=128)),
            writes=[fwv_sb], dma=True)
        for i in range(8):
            t0 = 512 * i
            fk2c = fk2c2[i % 2]
            stat = stat2[i % 2]
            B.op("sync", lambda e, t0=t0, fk2c=fk2c: e.dma_start(
                fk2c.ap, fk2_fm.rearrange("(c p) t -> p c t", p=128)[:, :, t0:t0 + 512]),
                reads=[HB["fk2_fm"]], writes=[fk2c], dma=True)
            B.op("sync", lambda e, t0=t0: e.dma_start(
                sfrc.ap, sfr_tm[t0:t0 + 512, :].rearrange("(q p) d -> p q d", p=128)),
                reads=[HB["sfr_tm"]], writes=[sfrc], dma=True)
            B.op("sync", lambda e, t0=t0: e.dma_start(
                scc.ap, s_tm[t0:t0 + 512, :].rearrange("(q p) d -> p q d", p=128)),
                reads=[HB["s_tm"]], writes=[scc], dma=True)
            for q in range(4):
                jobs = []
                for oh in range(2):
                    pb = pool.take()
                    jobs.append((pb, [fk2c.ap[:, fc, q * 128:(q + 1) * 128] for fc in range(FC)],
                                 [fwv_sb.ap[:, fc, oh * 512:(oh + 1) * 512] for fc in range(FC)]))
                mm_jobs(jobs, 512, [fk2c, fwv_sb])
                for oh in range(2):
                    pb = jobs[oh][0]
                    B.op("vector", lambda e, q=q, oh=oh, pb=pb: e.scalar_tensor_tensor(
                        y1p.ap[:, q, oh * 512:(oh + 1) * 512], pb.ap, 1.0,
                        sfrc.ap[:, q, oh * 512:(oh + 1) * 512], ALU.mult, ALU.mult),
                        reads=[pb, sfrc], writes=[y1p])
                    B.op("vector", lambda e, q=q, oh=oh: e.scalar_tensor_tensor(
                        y1p.ap[:, q, oh * 512:(oh + 1) * 512],
                        y1p.ap[:, q, oh * 512:(oh + 1) * 512], 1.0,
                        scc.ap[:, q, oh * 512:(oh + 1) * 512], ALU.mult, ALU.add),
                        reads=[y1p, scc], writes=[y1p])
            ln_stats(y1p, 4, stat)
            def napplyY(e, stat=stat):
                for q in range(4):
                    e.activation(y1t.ap[:, q, :], y1p.ap[:, q, :], AF.Identity,
                                 bias=stat.ap[:, 24 + q:25 + q],
                                 scale=stat.ap[:, 20 + q:21 + q])
                return e.nop()
            B.op("scalar", napplyY, reads=[y1p, stat], writes=[y1t])
            B.op("scalar", lambda e, t0=t0: e.dma_start(
                y1_tm[t0:t0 + 512, :].rearrange("(q p) d -> p q d", p=128), y1t.ap),
                reads=[y1t], writes=[HB["y1_tm"]], dma=True)
            for dc in range(DC):
                pb = pool.take()
                transpose_group(None, y1t,
                                [y1t.ap[:, q, dc * 128:(dc + 1) * 128] for q in range(4)], pb)
                B.op("scalar", lambda e, dc=dc, pb=pb: e.activation(
                    y1f.ap[:, dc, :], pb.ap[:, 0:512], AF.Copy),
                    reads=[pb], writes=[y1f])
            B.op("scalar", lambda e, t0=t0: e.dma_start(
                y1_fm.rearrange("(c p) t -> p c t", p=128)[:, :, t0:t0 + 512], y1f.ap),
                reads=[y1f], writes=[HB["y1_fm"]], dma=True)
    ar.release()
    B.barrier()

    ar.mark()
    wgT = ar.bf16([128, DC, 3072], "wgT")
    wuT = ar.bf16([128, DC, 3072], "wuT")
    y1fc2 = [ar.bf16([128, DC, 512], f"y1fc{j}") for j in range(2)]
    sgb = ar.bf16([128, HC, 512], "sgb")
    hb2_ = [ar.bf16([128, HC, 512], f"hbuf{j}") for j in range(2)]

    if debug_phases >= 5:
        for (srch, hbm, dstb, nch) in ((wg_bf, HB["wg_bf"], wgT, DC),
                                       (wu_bf, HB["wu_bf"], wuT, DC)):
            for c in range(nch):
                B.op("scalar", lambda e, srch=srch, dstb=dstb, c=c: e.dma_start_transpose(
                    dstb.ap[:, c, :], srch[:, c * 128:(c + 1) * 128]),
                    reads=[hbm], writes=[dstb], dma=True)
        for i in range(8):
            t0 = 512 * i
            y1fc, hbv = y1fc2[i % 2], hb2_[i % 2]
            B.op("sync", lambda e, t0=t0, y1fc=y1fc: e.dma_start(
                y1fc.ap, y1_fm.rearrange("(c p) t -> p c t", p=128)[:, :, t0:t0 + 512]),
                reads=[HB["y1_fm"]], writes=[y1fc], dma=True)
            for hh4 in range(0, HC, 2):
                jobs = []
                for hc in (hh4, hh4 + 1):
                    pb = pool.take()
                    jobs.append((pb, [wgT.ap[:, kc, hc * 128:(hc + 1) * 128] for kc in range(DC)],
                                 [y1fc.ap[:, kc, :] for kc in range(DC)]))
                for hc in (hh4, hh4 + 1):
                    pb = pool.take()
                    jobs.append((pb, [wuT.ap[:, kc, hc * 128:(hc + 1) * 128] for kc in range(DC)],
                                 [y1fc.ap[:, kc, :] for kc in range(DC)]))
                mm_jobs(jobs, 512, [wgT, wuT, y1fc])

                def silu2(e, jobs=jobs, hh4=hh4):
                    for k2, hc in enumerate((hh4, hh4 + 1)):
                        e.activation(sgb.ap[:, hc, :], jobs[k2][0].ap, AF.Silu, scale=S127)
                    return e.nop()
                B.op("scalar", silu2, reads=[jobs[0][0], jobs[1][0]], writes=[sgb])

                def hmul2(e, jobs=jobs, hh4=hh4, hbv=hbv):
                    for k2, hc in enumerate((hh4, hh4 + 1)):
                        e.scalar_tensor_tensor(hbv.ap[:, hc, :], jobs[2 + k2][0].ap, 1.0,
                                               sgb.ap[:, hc, :], ALU.mult, ALU.mult)
                    return e.nop()
                B.op("vector", hmul2, reads=[jobs[2][0], jobs[3][0], sgb], writes=[hbv])
            B.op("scalar", lambda e, t0=t0, hbv=hbv: e.dma_start(
                h_fm.rearrange("(c p) t -> p c t", p=128)[:, :, t0:t0 + 512],
                hbv.ap), reads=[hbv], writes=[HB["h_fm"]], dma=True)
    ar.release()
    B.barrier()

    ar.mark()
    wdT = ar.bf16([128, HC, 1024], "wdT")
    hc2 = [ar.bf16([128, HC, 512], f"hc{j}") for j in range(2)]
    y1tc = ar.f32([128, 4, 1024], "y1tc")
    y2p = ar.f32([128, 4, 1024], "y2p")

    if debug_phases >= 5:
        for c in range(HC):
            B.op("scalar", lambda e, c=c: e.dma_start_transpose(
                wdT.ap[:, c, :], wd_bf[:, c * 128:(c + 1) * 128]),
                reads=[HB["wd_bf"]], writes=[wdT], dma=True)
        for i in range(8):
            t0 = 512 * i
            hcv = hc2[i % 2]
            stat = stat2[i % 2]
            B.op("sync", lambda e, t0=t0, hcv=hcv: e.dma_start(
                hcv.ap, h_fm.rearrange("(c p) t -> p c t", p=128)[:, :, t0:t0 + 512]),
                reads=[HB["h_fm"]], writes=[hcv], dma=True)
            B.op("sync", lambda e, t0=t0: e.dma_start(
                y1tc.ap, y1_tm[t0:t0 + 512, :].rearrange("(q p) d -> p q d", p=128)),
                reads=[HB["y1_tm"]], writes=[y1tc], dma=True)
            for qb in range(0, 4, 2):
                jobs = []
                for q in (qb, qb + 1):
                    for oh in range(2):
                        pb = pool.take()
                        jobs.append((pb, [hcv.ap[:, hc, q * 128:(q + 1) * 128] for hc in range(HC)],
                                     [wdT.ap[:, hc, oh * 512:(oh + 1) * 512] for hc in range(HC)]))
                mm_jobs(jobs, 512, [hcv, wdT])
                ji = 0
                for q in (qb, qb + 1):
                    for oh in range(2):
                        pb = jobs[ji][0]; ji += 1
                        B.op("vector", lambda e, q=q, oh=oh, pb=pb: e.scalar_tensor_tensor(
                            y2p.ap[:, q, oh * 512:(oh + 1) * 512], pb.ap, S127 * S127,
                            y1tc.ap[:, q, oh * 512:(oh + 1) * 512], ALU.mult, ALU.add),
                            reads=[pb, y1tc], writes=[y2p])
            ln_stats(y2p, 4, stat)

            def napplyD(e, stat=stat):
                for q in range(4):
                    e.activation(y1tc.ap[:, q, :], y2p.ap[:, q, :], AF.Identity,
                                 bias=stat.ap[:, 24 + q:25 + q],
                                 scale=stat.ap[:, 20 + q:21 + q])
                return e.nop()
            B.op("scalar", napplyD, reads=[y2p, stat], writes=[y1tc])
            B.op("sync", lambda e, t0=t0: e.dma_start(
                out_h[t0:t0 + 512, :].rearrange("(q p) d -> p q d", p=128), y1tc.ap),
                reads=[y1tc], dma=True)
    ar.release()

    for ds in B.dsems["sync"] + B.dsems["scalar"]:
        B.raw("sync", lambda e, ds=ds: e.wait_ge(ds.h, ds.n))
    for ds in B.dsems["gpsimd"]:
        B.raw("gpsimd", lambda e, ds=ds: e.wait_ge(ds.h, ds.n))

    with nc.Block() as blk:
        @blk.sync
        def _(e):
            for fn in B.ops["sync"]:
                fn(e)

        @blk.gpsimd
        def _(e):
            for fn in B.ops["gpsimd"]:
                fn(e)

        @blk.tensor
        def _(e):
            for fn in B.ops["tensor"]:
                fn(e)

        @blk.vector
        def _(e):
            for fn in B.ops["vector"]:
                fn(e)

        @blk.scalar
        def _(e):
            for fn in B.ops["scalar"]:
                fn(e)

    stack.close()
    return nc


_IN_MAP = {
    "wk": "L0_attn_w_key", "wv": "L0_attn_w_value", "wr": "L0_attn_w_recept",
    "wo": "L0_attn_w_output", "fwk": "L0_attn_f_w_key", "fwr": "L0_attn_f_w_recept",
    "fwv": "L0_attn_f_w_value", "wg": "L0_ffn_w_gate", "wu": "L0_ffn_w_up",
    "wd": "L0_ffn_w_down", "ln1w": "L0_attn_ln1_w", "ln1b": "L0_attn_ln1_b",
    "ln2w": "L0_attn_ln2_w", "ln2b": "L0_attn_ln2_b", "tdec": "L0_attn_time_decay",
    "tfir": "L0_attn_time_first", "tmk": "L0_attn_time_mix_k",
    "tmv": "L0_attn_time_mix_v", "tmr": "L0_attn_time_mix_r",
    "ftmk": "L0_attn_f_time_mix_k", "ftmr": "L0_attn_f_time_mix_r",
}


def kernel(**inputs):
    nc = build(DEBUG_PHASES, DEBUG_OUTS)
    shared = {}
    for k, src in _IN_MAP.items():
        a = np.ascontiguousarray(np.asarray(inputs[src]))
        if a.ndim == 3 and a.shape[0] == 1:
            a = a.reshape(-1)
        shared[k] = a
    x_full = np.ascontiguousarray(np.asarray(inputs["L0_attn_x"]))
    in_maps = []
    for b in range(8):
        m = dict(shared)
        m["x"] = x_full[b]
        in_maps.append(m)
    res = run_bass_kernel_spmd(nc, in_maps, core_ids=list(range(8)))
    return np.stack([r["out"] for r in res.results], axis=0)


# version 17
# speedup vs baseline: 1.0375x; 1.0341x over previous
import numpy as np

import concourse.bass as bass
import concourse.mybir as mybir
from concourse.bass_utils import run_bass_kernel_spmd

F32 = mybir.dt.float32
BF16 = mybir.dt.bfloat16
I32 = mybir.dt.int32
AF = mybir.ActivationFunctionType
ALU = mybir.AluOpType
AX = mybir.AxisListType

T, D, FH, HH = 4096, 1024, 4096, 3072
DC = D // 128
FC = FH // 128
HC = HH // 128
S127 = 1.0 / 127.0
EPS = 1e-5

DEBUG_PHASES = 99
DEBUG_OUTS = ()

ENGS = ("sync", "gpsimd", "tensor", "vector", "scalar")


class Sem:
    def __init__(self, name, eng=None):
        self.name, self.n, self.h, self.eng = name, 0, None, eng


class Buf:

    def __init__(self, ap, name=""):
        self.ap = ap
        self.name = name
        self.w = None
        self.readers = []


class Builder:
    def __init__(self, nc):
        self.nc = nc
        self.ops = {e: [] for e in ENGS}
        self.sems = {e: Sem("sem_" + e, e) for e in ENGS}
        self.dsems = {q: [Sem(f"d_{q}{k}") for k in range(8)]
                      for q in ("sync", "gpsimd", "scalar")}
        self.didx = {q: 0 for q in ("sync", "gpsimd", "scalar")}
        self.waited = {e: {} for e in ENGS}
        self.force_drain = True

    def all_sems(self):
        out = list(self.sems.values())
        for q in self.dsems.values():
            out += q
        return out

    def _wait(self, eng, tok):
        sem, val, src, flag = tok
        if src == eng:
            return
        prev = self.waited[eng].get(sem.name, -1)
        if val <= prev:
            return
        self.waited[eng][sem.name] = val
        flag["x"] = True
        self.ops[eng].append(lambda e, s=sem, v=val: e.wait_ge(s.h, v))

    def op(self, eng, build, reads=(), writes=(), dma=False):
        if dma:
            for b in list(reads) + list(writes):
                if b.w is not None and b.w[2] == eng:
                    b.w[3]["x"] = True
                for t in b.readers:
                    if t[2] == eng:
                        t[3]["x"] = True
        for b in reads:
            if b.w is not None:
                self._wait(eng, b.w)
        for b in writes:
            if b.w is not None:
                self._wait(eng, b.w)
            for t in b.readers:
                self._wait(eng, t)
        if dma:
            sem = self.dsems[eng][self.didx[eng] % 8]
            self.didx[eng] += 1
            if sem.n > 0:
                self._wait(eng, (sem, sem.n, None, {"x": True}))
            sem.n += 16
            step = 16
        else:
            sem = self.sems[eng]
            sem.n += 1
            step = 1
        val = sem.n
        flag = {"x": self.force_drain}
        tok = (sem, val, None if dma else eng, flag)

        if dma or eng == "sync":
            def fn(e, build=build, sem=sem, step=step):
                ins = build(e)
                ins.then_inc(sem.h, step)
        else:
            def fn(e, build=build, sem=sem, step=step, flag=flag):
                ins = build(e)
                if flag["x"]:
                    e.drain().then_inc(sem.h, step)
                else:
                    ins.then_inc(sem.h, step)

        self.ops[eng].append(fn)
        for b in reads:
            b.readers.append(tok)
        for b in writes:
            b.w = tok
            b.readers = []
        return tok

    def raw(self, eng, fn):
        self.ops[eng].append(fn)

    def barrier(self):
        toks = []
        for e in ("gpsimd", "tensor", "vector", "scalar"):
            sem = self.sems[e]
            sem.n += 1
            self.ops[e].append(lambda eng, s=sem: eng.drain().then_inc(s.h, 1))
            toks.append((sem, sem.n, e, {"x": True}))
        for q in self.dsems.values():
            for ds in q:
                toks.append((ds, ds.n, None, {"x": True}))
        for e in ENGS:
            for t in toks:
                self._wait(e, t)


class Arena:

    def __init__(self, sb, cols):
        self.sb, self.cols, self.top, self.marks = sb, cols, 0, []

    def mark(self):
        self.marks.append(self.top)

    def release(self):
        self.top = self.marks.pop()

    def alloc(self, ncols, name=""):
        a = (self.top + 7) & ~7
        self.top = a + ncols
        assert self.top <= self.cols, f"arena overflow at {name}: {self.top}>{self.cols}"
        return a

    def f32(self, shape, name=""):
        n = int(np.prod(shape[1:]))
        a = self.alloc(n, name)
        ap = self.sb[:, a:a + n]
        if len(shape) > 2:
            ap = ap.rearrange("p (a b) -> p a b", a=shape[1])
        return Buf(ap, name)

    def bf16(self, shape, name=""):
        n = int(np.prod(shape[1:]))
        a = self.alloc((n + 1) // 2, name)
        ap = self.sb[:, a:a + (n + 1) // 2].bitcast(BF16)
        if len(shape) > 2:
            ap = ap.rearrange("p (a b) -> p a b", a=shape[1])
        return Buf(ap, name)


class PsumPool:
    def __init__(self, ps):
        self.banks = [Buf(ps[:, b, :], f"psum{b}") for b in range(8)]
        self.i = 0

    def take(self):
        b = self.banks[self.i]
        self.i = (self.i + 1) % 8
        return b


def build(debug_phases=99, debug_outs=()):
    nc = bass.Bass("TRN2", target_bir_lowering=False)

    def dram(name, shape, dtype, kind):
        if name in debug_outs:
            kind = "ExternalOutput"
        return nc.dram_tensor(name, shape, dtype, kind=kind).ap()

    x_h = dram("x", [T, D], F32, "ExternalInput")
    wk_h = dram("wk", [D, D], F32, "ExternalInput")
    wv_h = dram("wv", [D, D], F32, "ExternalInput")
    wr_h = dram("wr", [D, D], F32, "ExternalInput")
    wo_h = dram("wo", [D, D], F32, "ExternalInput")
    fwk_h = dram("fwk", [D, FH], F32, "ExternalInput")
    fwr_h = dram("fwr", [D, D], F32, "ExternalInput")
    fwv_h = dram("fwv", [FH, D], F32, "ExternalInput")
    wg_h = dram("wg", [HH, D], I32, "ExternalInput")
    wu_h = dram("wu", [HH, D], I32, "ExternalInput")
    wd_h = dram("wd", [D, HH], I32, "ExternalInput")
    pv_h = {}
    for p in ("ln1w", "ln1b", "ln2w", "ln2b", "tdec", "tfir",
              "tmk", "tmv", "tmr", "ftmk", "ftmr"):
        pv_h[p] = dram(p, [D], F32, "ExternalInput")
    out_h = dram("out", [T, D], F32, "ExternalOutput")
    dbg_h = dram("dbg", [128, 16384], F32, "Internal")

    k_fm = dram("k_fm", [D, T], F32, "Internal")
    v_fm = dram("v_fm", [D, T], F32, "Internal")
    r_fm = dram("r_fm", [D, T], F32, "Internal")
    y_fm = dram("y_fm", [D, T], BF16, "Internal")
    s_tm = dram("s_tm", [T, D], F32, "Internal")
    fk2_fm = dram("fk2_fm", [FH, T], BF16, "Internal")
    sfr_tm = dram("sfr_tm", [T, D], BF16, "Internal")
    y1_tm = dram("y1_tm", [T, D], F32, "Internal")
    y1_fm = dram("y1_fm", [D, T], BF16, "Internal")
    wg_bf = dram("wg_bf", [HH, D], BF16, "Internal")
    wu_bf = dram("wu_bf", [HH, D], BF16, "Internal")
    wd_bf = dram("wd_bf", [D, HH], BF16, "Internal")
    h_fm = dram("h_fm", [HH, T], BF16, "Internal")
    HB = {n: Buf(None, "HB_" + n) for n in
          ("k_fm", "v_fm", "r_fm", "y_fm", "s_tm", "fk2_fm", "sfr_tm",
           "y1_tm", "y1_fm", "wg_bf", "wu_bf", "wd_bf", "h_fm")}

    ARENA_COLS = 52224
    import contextlib
    stack = contextlib.ExitStack()
    sb = stack.enter_context(nc.sbuf_tensor("arena", [128, ARENA_COLS], F32))
    ps = stack.enter_context(nc.psum_tensor("psum", [128, 8, 512], F32))

    B = Builder(nc)
    for s in B.all_sems():
        s.h = stack.enter_context(nc.semaphore(s.name))

    ar = Arena(sb, ARENA_COLS)
    pool = PsumPool(ps)

    ident = ar.f32([128, 128], "ident")
    P = {p: ar.f32([128, DC], p) for p in pv_h}
    dw = ar.f32([128, DC], "dw")
    eu = ar.f32([128, DC], "eu")
    mhalf = ar.f32([128, 8], "mhalf")
    epsb = ar.f32([128, 1], "epsb")
    stat2 = [ar.f32([128, 32], f"stat{j}") for j in range(2)]
    sq = ar.f32([128, 1024], "sq")

    B.op("gpsimd", lambda e: e.memset(ident.ap, 0.0), writes=[ident])
    B.op("gpsimd", lambda e: e.affine_select(
        out=ident.ap, in_=ident.ap, compare_op=ALU.not_equal, fill=1.0,
        base=0, pattern=[[-1, 128]], channel_multiplier=1),
        reads=[ident], writes=[ident])
    B.op("gpsimd", lambda e: e.memset(mhalf.ap, -0.5), writes=[mhalf])
    B.op("gpsimd", lambda e: e.memset(epsb.ap, EPS), writes=[epsb])
    B.op("gpsimd", lambda e: e.memset(stat2[0].ap, 0.0), writes=[stat2[0]])
    B.op("gpsimd", lambda e: e.memset(stat2[1].ap, 0.0), writes=[stat2[1]])
    B.op("gpsimd", lambda e: e.memset(sq.ap, 0.0), writes=[sq])
    for p, h in pv_h.items():
        B.op("gpsimd", lambda e, h=h, p=p: e.dma_start(
            P[p].ap, h.rearrange("(c p) -> p c", p=128),
            allow_slow_non_contiguous=True), writes=[P[p]], dma=True)
    B.op("scalar", lambda e: e.activation(dw.ap, P["tdec"].ap, AF.Exp),
         reads=[P["tdec"]], writes=[dw])
    B.op("scalar", lambda e: e.activation(dw.ap, dw.ap, AF.Exp, scale=-1.0),
         reads=[dw], writes=[dw])
    B.op("scalar", lambda e: e.activation(eu.ap, P["tfir"].ap, AF.Exp),
         reads=[P["tfir"]], writes=[eu])

    _dq_jobs = []
    for (src, dsth, dstbn, nch) in ((wg_h, wg_bf, "wg_bf", HC),
                                    (wu_h, wu_bf, "wu_bf", HC),
                                    (wd_h, wd_bf, "wd_bf", DC)):
        for c in range(nch):
            _dq_jobs.append((src, dsth, dstbn, c))

    def emit_ffn_dequant(frac_lo, frac_hi):
        lo = int(len(_dq_jobs) * frac_lo)
        hi = int(len(_dq_jobs) * frac_hi)
        for idx in range(lo, hi):
            src, dsth, dstbn, c = _dq_jobs[idx]
            width = src.shape[1]
            stgb = wstg[idx % 2]
            B.op("gpsimd", lambda e, src=src, c=c, stgb=stgb, width=width:
                 e.dma_start(stgb.ap[:, 0:width], src[c * 128:(c + 1) * 128, :]),
                 writes=[stgb], dma=True)
            B.op("gpsimd", lambda e, dsth=dsth, c=c, stgb=stgb, width=width:
                 e.dma_start(dsth[c * 128:(c + 1) * 128, :], stgb.ap[:, 0:width]),
                 reads=[stgb], writes=[HB[dstbn]], dma=True)

    def ln_stats(xbuf, nsub, stat):
        xa = xbuf.ap

        def reds(e, nsub=nsub, xa=xa, stat=stat):
            for q in range(nsub):
                e.reduce_sum(stat.ap[:, q:q + 1], xa[:, q, :], AX.X)
            return e.nop()
        B.op("vector", reds, reads=[xbuf], writes=[stat])

        def sqs(e, nsub=nsub, xa=xa, stat=stat):
            for q in range(nsub):
                e.activation(sq.ap, xa[:, q, :], AF.Square,
                             accum_out=stat.ap[:, 4 + q:5 + q])
            return e.nop()
        B.op("scalar", sqs, reads=[xbuf], writes=[stat, sq])
        B.op("scalar", lambda e: e.activation(
            stat.ap[:, 8:8 + nsub], stat.ap[:, 0:nsub], AF.Identity, scale=1.0 / D),
            reads=[stat], writes=[stat])
        B.op("scalar", lambda e: e.activation(
            stat.ap[:, 12:12 + nsub], stat.ap[:, 8:8 + nsub], AF.Square),
            reads=[stat], writes=[stat])
        B.op("scalar", lambda e: e.activation(
            stat.ap[:, 28:28 + nsub], stat.ap[:, 4:4 + nsub], AF.Identity,
            scale=1.0 / D, bias=epsb.ap),
            reads=[stat, epsb], writes=[stat])
        B.op("gpsimd", lambda e: e.tensor_tensor(
            stat.ap[:, 16:16 + nsub], stat.ap[:, 28:28 + nsub],
            stat.ap[:, 12:12 + nsub], ALU.subtract),
            reads=[stat], writes=[stat])
        B.op("gpsimd", lambda e: e.tensor_scalar(
            stat.ap[:, 16:16 + nsub], stat.ap[:, 16:16 + nsub], 1e-6, None, ALU.max),
            reads=[stat], writes=[stat])
        B.op("gpsimd", lambda e: e.tensor_tensor(
            stat.ap[:, 20:20 + nsub], stat.ap[:, 16:16 + nsub],
            mhalf.ap[:, 0:nsub], ALU.pow),
            reads=[stat, mhalf], writes=[stat])
        B.op("scalar", lambda e: e.activation(
            stat.ap[:, 12:12 + nsub], stat.ap[:, 8:8 + nsub], AF.Identity, scale=-1.0),
            reads=[stat], writes=[stat])
        B.op("gpsimd", lambda e: e.tensor_tensor(
            stat.ap[:, 24:24 + nsub], stat.ap[:, 12:12 + nsub],
            stat.ap[:, 20:20 + nsub], ALU.mult),
            reads=[stat, mhalf], writes=[stat])

    def mm_tile(psumb, n, lhsT_buf, lhsT_aps, rhs_buf, rhs_aps):
        mm_jobs([(psumb, lhsT_aps, rhs_aps)], n, [lhsT_buf, rhs_buf])

    def mm_jobs(jobs, n, reads):

        def grp(e):
            last = None
            for psumb, lts, rhs in jobs:
                cnt = len(lts)
                for i in range(cnt):
                    last = e.matmul(psumb.ap[:, 0:n], lts[i], rhs[i],
                                    start=(i == 0), stop=(i == cnt - 1))
            return last
        B.op("tensor", grp, reads=reads, writes=[j[0] for j in jobs])

    def transpose_group(dcs_qs, src_buf, src_aps, bank):

        def grp(e):
            last = None
            for idx, ap in enumerate(src_aps):
                last = e.transpose(bank.ap[:, idx * 128:(idx + 1) * 128], ap, ident.ap)
            return last
        B.op("tensor", grp, reads=[src_buf, ident], writes=[bank])

    ar.mark()
    wstg = [ar.bf16([128, 3072], "wstg0")] * 2
    wk_sb = ar.bf16([128, DC, 1024], "wk_sb")
    wv_sb = ar.bf16([128, DC, 1024], "wv_sb")
    wr_sb = ar.bf16([128, DC, 1024], "wr_sb")
    x_t2 = [ar.f32([128, 4, 1024], f"x_t{j}") for j in range(2)]
    n_t = ar.f32([128, 4, 1024], "n_t")
    xn = ar.f32([128, DC, 513], "xn")
    dnc = ar.f32([128, 512], "dnc")
    m_k2 = [ar.bf16([128, DC, 512], f"m_k{j}") for j in range(2)]
    m_v2 = [ar.bf16([128, DC, 512], f"m_v{j}") for j in range(2)]
    m_r2 = [ar.bf16([128, DC, 512], f"m_r{j}") for j in range(2)]
    stg = {p: [ar.f32([128, 512], f"stg_{p}{j}") for j in range(4)]
           for p in ("k", "v", "r")}

    for w_h, w_sb in ((wk_h, wk_sb), (wv_h, wv_sb), (wr_h, wr_sb)):
        B.op("gpsimd", lambda e, w_h=w_h, w_sb=w_sb: e.dma_start(
            w_sb.ap, w_h.rearrange("(c p) o -> p c o", p=128)),
            writes=[w_sb], dma=True)
    B.op("gpsimd", lambda e: e.memset(xn.ap[:, :, 0:1], 0.0), writes=[xn])

    def emit_projA(i):
        t0 = 512 * i
        m_k, m_v, m_r = m_k2[i % 2], m_v2[i % 2], m_r2[i % 2]
        for name, w_sb, m, dst, hb in (("k", wk_sb, m_k, k_fm, HB["k_fm"]),
                                       ("v", wv_sb, m_v, v_fm, HB["v_fm"]),
                                       ("r", wr_sb, m_r, r_fm, HB["r_fm"])):
            for ob in range(0, DC, 4):
                jobs = []
                for oc in range(ob, ob + 4):
                    pb = pool.take()
                    jobs.append((pb, [w_sb.ap[:, kc, oc * 128:(oc + 1) * 128] for kc in range(DC)],
                                 [m.ap[:, kc, :] for kc in range(DC)]))
                mm_jobs(jobs, 512, [w_sb, m])
                sgs = [stg[name][oc % 4] for oc in range(ob, ob + 4)]

                def evac4(e, jobs=jobs, sgs=sgs):
                    for k4 in range(4):
                        e.activation(sgs[k4].ap, jobs[k4][0].ap, AF.Copy)
                    return e.nop()
                B.op("scalar", evac4, reads=[j[0] for j in jobs], writes=sgs)
                for k4, oc in enumerate(range(ob, ob + 4)):
                    B.op("scalar", lambda e, dst=dst, oc=oc, t0=t0, sg_=sgs[k4]: e.dma_start(
                        dst[oc * 128:(oc + 1) * 128, t0:t0 + 512], sg_.ap),
                        reads=[sgs[k4]], writes=[hb], dma=True)

    if debug_phases >= 1:
        for i in range(8):
            t0 = 512 * i
            x_t = x_t2[i % 2]
            m_k, m_v, m_r = m_k2[i % 2], m_v2[i % 2], m_r2[i % 2]
            stat = stat2[i % 2]
            B.op("sync", lambda e, t0=t0, x_t=x_t: e.dma_start(
                x_t.ap, x_h[t0:t0 + 512, :].rearrange("(q p) d -> p q d", p=128)),
                writes=[x_t], dma=True)
            ln_stats(x_t, 4, stat)
            def napplyA(e, n_t=n_t, x_t=x_t, stat=stat):
                for q in range(4):
                    e.activation(n_t.ap[:, q, :], x_t.ap[:, q, :], AF.Identity,
                                 bias=stat.ap[:, 24 + q:25 + q],
                                 scale=stat.ap[:, 20 + q:21 + q])
                return e.nop()
            B.op("scalar", napplyA, reads=[x_t, stat], writes=[n_t])
            for dc in range(DC):
                pb = pool.take()
                transpose_group(None, n_t,
                                [n_t.ap[:, q, dc * 128:(dc + 1) * 128] for q in range(4)], pb)
                B.op("scalar", lambda e, dc=dc, pb=pb: e.activation(
                    xn.ap[:, dc, 1:513], pb.ap[:, 0:512],
                    AF.Identity, bias=P["ln1b"].ap[:, dc:dc + 1],
                    scale=P["ln1w"].ap[:, dc:dc + 1]),
                    reads=[pb, P["ln1w"], P["ln1b"]], writes=[xn])
            for dc in range(DC):
                B.op("vector", lambda e, dc=dc: e.scalar_tensor_tensor(
                    dnc.ap, xn.ap[:, dc, 1:513], 1.0, xn.ap[:, dc, 0:512],
                    ALU.mult, ALU.subtract),
                    reads=[xn], writes=[dnc])

                def mixes(e, dc=dc, m_k=m_k, m_v=m_v, m_r=m_r):
                    for tm, m in (("tmk", m_k), ("tmv", m_v), ("tmr", m_r)):
                        e.scalar_tensor_tensor(
                            m.ap[:, dc, :], dnc.ap, P[tm].ap[:, dc:dc + 1],
                            xn.ap[:, dc, 0:512], ALU.mult, ALU.add)
                    return e.nop()
                B.op("vector", mixes,
                     reads=[dnc, xn, P["tmk"], P["tmv"], P["tmr"]],
                     writes=[m_k, m_v, m_r])
            B.op("gpsimd", lambda e: e.tensor_copy(xn.ap[:, :, 0:1], xn.ap[:, :, 512:513]),
                 reads=[xn], writes=[xn])
            if i >= 1:
                emit_projA(i - 1)
            emit_ffn_dequant(i / 8.0, (i + 1) / 8.0)
        emit_projA(7)
    ar.release()
    B.barrier()

    ar.mark()
    kb2 = [ar.f32([128, 4096], f"kb{j}") for j in range(2)]
    vb2 = [ar.f32([128, 4096], f"vb{j}") for j in range(2)]
    rb2 = [ar.f32([128, 4096], f"rb{j}") for j in range(2)]
    ek = ar.f32([128, 4096], "ek")
    ekv = ar.f32([128, 4096], "ekv")
    tmp = ar.f32([128, 4096], "tmp")
    Abuf = ar.f32([128, 4100], "Abuf")
    Bbuf = ar.f32([128, 4100], "Bbuf")
    yb2 = [ar.bf16([128, 4096], f"yb{j}") for j in range(2)]
    kmax2 = [ar.f32([128, 2], f"kmax{j}") for j in range(2)]

    B.force_drain = True
    B.op("gpsimd", lambda e: e.memset(Abuf.ap[:, 0:1], 0.0), writes=[Abuf])
    B.op("gpsimd", lambda e: e.memset(Bbuf.ap[:, 0:1], 0.0), writes=[Bbuf])

    if debug_phases >= 2:
        for j in range(DC):
            kb, vb, rb = kb2[j % 2], vb2[j % 2], rb2[j % 2]
            yb, kmax = yb2[j % 2], kmax2[j % 2]
            for src, hb, b in ((k_fm, HB["k_fm"], kb), (v_fm, HB["v_fm"], vb),
                               (r_fm, HB["r_fm"], rb)):
                B.op("sync", lambda e, src=src, j=j, b=b: e.dma_start(
                    b.ap, src[j * 128:(j + 1) * 128, :]),
                    reads=[hb], writes=[b], dma=True)
            if j == 7 and "dbg" in debug_outs:
                B.op("sync", lambda e: e.dma_start(dbg_h[:, 0:4096], rb.ap),
                     reads=[rb], dma=True)
            B.op("scalar", lambda e, rb=rb: e.activation(
                tmp.ap, rb.ap, AF.Exp, scale=-1.0), reads=[rb], writes=[tmp])
            B.op("vector", lambda e, kmax=kmax, kb=kb: e.reduce_max(
                kmax.ap[:, 0:1], kb.ap[:, 0:512], AX.X), reads=[kb], writes=[kmax])
            B.op("vector", lambda e, kmax=kmax: e.tensor_scalar(
                kmax.ap[:, 1:2], kmax.ap[:, 0:1], -1.0, -20.0, ALU.mult, ALU.max),
                reads=[kmax], writes=[kmax])
            B.op("vector", lambda e, kmax=kmax: e.tensor_scalar(
                kmax.ap[:, 1:2], kmax.ap[:, 1:2], 20.0, None, ALU.min),
                reads=[kmax], writes=[kmax])
            B.op("scalar", lambda e, kb=kb, kmax=kmax: e.activation(
                ek.ap, kb.ap, AF.Exp, bias=kmax.ap[:, 1:2]),
                reads=[kb, kmax], writes=[ek])
            B.op("vector", lambda e, vb=vb: e.scalar_tensor_tensor(
                ekv.ap, ek.ap, 1.0, vb.ap, ALU.mult, ALU.mult),
                reads=[ek, vb], writes=[ekv])
            B.op("vector", lambda e, j=j: e.tensor_tensor_scan(
                Abuf.ap[:, 1:4097], dw.ap[:, j:j + 1].to_broadcast([128, 4096]),
                ekv.ap, 0.0, ALU.mult, ALU.add),
                reads=[ekv, dw], writes=[Abuf])
            B.op("vector", lambda e, j=j: e.tensor_tensor_scan(
                Bbuf.ap[:, 1:4097], dw.ap[:, j:j + 1].to_broadcast([128, 4096]),
                ek.ap, 0.0, ALU.mult, ALU.add),
                reads=[ek, dw], writes=[Bbuf])
            B.op("vector", lambda e, j=j, kb=kb: e.scalar_tensor_tensor(
                kb.ap, ekv.ap, eu.ap[:, j:j + 1], Abuf.ap[:, 0:4096],
                ALU.mult, ALU.add),
                reads=[ekv, eu, Abuf], writes=[kb])
            B.op("vector", lambda e, j=j, vb=vb: e.scalar_tensor_tensor(
                vb.ap, ek.ap, eu.ap[:, j:j + 1], Bbuf.ap[:, 0:4096],
                ALU.mult, ALU.add),
                reads=[ek, eu, Bbuf], writes=[vb])
            B.op("vector", lambda e, vb=vb: e.scalar_tensor_tensor(
                ekv.ap, tmp.ap, 1.0, vb.ap, ALU.add, ALU.mult),
                reads=[tmp, vb], writes=[ekv])
            B.op("scalar", lambda e: e.activation(
                tmp.ap, ekv.ap, AF.Ln), reads=[ekv], writes=[tmp])
            B.op("scalar", lambda e: e.activation(
                tmp.ap, tmp.ap, AF.Exp, scale=-1.0), reads=[tmp], writes=[tmp])
            if j == 7 and "dbg" in debug_outs:
                B.op("sync", lambda e: e.dma_start(dbg_h[:, 4096:8192], Abuf.ap[:, 0:4096]),
                     reads=[Abuf], dma=True)
                B.op("sync", lambda e: e.dma_start(dbg_h[:, 8192:12288], kb.ap),
                     reads=[kb], dma=True)
                B.op("sync", lambda e: e.dma_start(dbg_h[:, 12288:12290], kmax.ap),
                     reads=[kmax], dma=True)
                B.op("sync", lambda e: e.dma_start(
                    dbg_h[:, 12296:14344].bitcast(BF16), yb.ap), reads=[yb], dma=True)

            B.op("vector", lambda e, yb=yb, kb=kb: e.scalar_tensor_tensor(
                yb.ap, kb.ap, 1.0, tmp.ap, ALU.mult, ALU.mult),
                reads=[kb, tmp], writes=[yb])
            B.op("scalar", lambda e, j=j, yb=yb: e.dma_start(
                y_fm[j * 128:(j + 1) * 128, :], yb.ap),
                reads=[yb], writes=[HB["y_fm"]], dma=True)
    B.force_drain = False
    ar.release()
    B.barrier()

    ar.mark()
    wo_sb = ar.bf16([128, DC, 1024], "wo_sb")
    fwk_sb = ar.bf16([128, DC, 4096], "fwk_sb")
    fwr_sb = ar.bf16([128, DC, 1024], "fwr_sb")
    xc2 = [ar.f32([128, 2, 1024], f"xc{j}") for j in range(2)]
    yc2 = [ar.bf16([128, DC, 256], f"yc{j}") for j in range(2)]
    x1 = ar.f32([128, 2, 1024], "x1")
    sc = ar.f32([128, 2, 1024], "sc")
    n2 = ar.f32([128, 2, 1024], "n2")
    xn2 = ar.f32([128, DC, 257], "xn2")
    dn2 = ar.f32([128, 256], "dn2")
    m2k2 = [ar.bf16([128, DC, 256], f"m2k{j}") for j in range(2)]
    m2r2 = [ar.bf16([128, DC, 256], f"m2r{j}") for j in range(2)]
    fkst = ar.bf16([128, FC, 256], "fkst")
    relu_t = ar.f32([128, 1024], "relu_t")
    sfrst = ar.bf16([128, 2, 1024], "sfrst")

    if debug_phases >= 3:
        for w_h, w_sb in ((wo_h, wo_sb), (fwk_h, fwk_sb), (fwr_h, fwr_sb)):
            B.op("gpsimd", lambda e, w_h=w_h, w_sb=w_sb: e.dma_start(
                w_sb.ap, w_h.rearrange("(c p) o -> p c o", p=128)),
                writes=[w_sb], dma=True)
        B.op("gpsimd", lambda e: e.memset(xn2.ap[:, :, 0:1], 0.0), writes=[xn2])

        def emit_fkfr(i):
            t0 = 256 * i
            m2k, m2r = m2k2[i % 2], m2r2[i % 2]
            for fb in range(0, FC, 4):
                jobs = []
                for fc in range(fb, fb + 4):
                    pb = pool.take()
                    jobs.append((pb, [fwk_sb.ap[:, kc, fc * 128:(fc + 1) * 128] for kc in range(DC)],
                                 [m2k.ap[:, kc, :] for kc in range(DC)]))
                mm_jobs(jobs, 256, [fwk_sb, m2k])

                def relu4(e, jobs=jobs):
                    for k4 in range(4):
                        e.tensor_scalar(relu_t.ap[:, k4 * 256:(k4 + 1) * 256],
                                        jobs[k4][0].ap[:, 0:256], 0.0, None, ALU.max)
                    return e.nop()
                B.op("vector", relu4, reads=[j[0] for j in jobs], writes=[relu_t])
                B.op("scalar", lambda e, fb=fb, fkst=fkst: e.activation(
                    fkst.ap[:, fb:fb + 4, :].rearrange("p a b -> p (a b)"),
                    relu_t.ap, AF.Square),
                    reads=[relu_t], writes=[fkst])
            B.op("scalar", lambda e, t0=t0, fkst=fkst: e.dma_start(
                fk2_fm.rearrange("(c p) t -> p c t", p=128)[:, :, t0:t0 + 256], fkst.ap),
                reads=[fkst], writes=[HB["fk2_fm"]], dma=True)
            for q in range(2):
                for oh in range(2):
                    pb = pool.take()
                    mm_tile(pb, 512, m2r,
                            [m2r.ap[:, kc, q * 128:(q + 1) * 128] for kc in range(DC)],
                            fwr_sb,
                            [fwr_sb.ap[:, kc, oh * 512:(oh + 1) * 512] for kc in range(DC)])
                    B.op("scalar", lambda e, q=q, oh=oh, pb=pb: e.activation(
                        sfrst.ap[:, q, oh * 512:(oh + 1) * 512], pb.ap, AF.Sigmoid),
                        reads=[pb], writes=[sfrst])
            B.op("scalar", lambda e, t0=t0: e.dma_start(
                sfr_tm[t0:t0 + 256, :].rearrange("(q p) d -> p q d", p=128), sfrst.ap),
                reads=[sfrst], writes=[HB["sfr_tm"]], dma=True)

        for i in range(16):
            t0 = 256 * i
            xc, yc = xc2[i % 2], yc2[i % 2]
            m2k, m2r = m2k2[i % 2], m2r2[i % 2]
            stat = stat2[i % 2]
            B.op("sync", lambda e, t0=t0, xc=xc: e.dma_start(
                xc.ap, x_h[t0:t0 + 256, :].rearrange("(q p) d -> p q d", p=128)),
                writes=[xc], dma=True)
            B.op("sync", lambda e, t0=t0, yc=yc: e.dma_start(
                yc.ap, y_fm.rearrange("(c p) t -> p c t", p=128)[:, :, t0:t0 + 256]),
                reads=[HB["y_fm"]], writes=[yc], dma=True)
            jobs = []
            for q in range(2):
                for oh in range(2):
                    pb = pool.take()
                    jobs.append((pb, [yc.ap[:, kc, q * 128:(q + 1) * 128] for kc in range(DC)],
                                 [wo_sb.ap[:, kc, oh * 512:(oh + 1) * 512] for kc in range(DC)]))
            mm_jobs(jobs, 512, [yc, wo_sb])
            ji = 0
            for q in range(2):
                for oh in range(2):
                    pb = jobs[ji][0]; ji += 1
                    B.op("vector", lambda e, q=q, oh=oh, pb=pb, xc=xc: e.scalar_tensor_tensor(
                        x1.ap[:, q, oh * 512:(oh + 1) * 512], pb.ap, 1.0,
                        xc.ap[:, q, oh * 512:(oh + 1) * 512], ALU.mult, ALU.add),
                        reads=[pb, xc], writes=[x1])
            for q in range(2):
                B.op("vector", lambda e, q=q, xc=xc: e.scalar_tensor_tensor(
                    sc.ap[:, q, :], x1.ap[:, q, :], 1.0, xc.ap[:, q, :],
                    ALU.mult, ALU.add), reads=[x1, xc], writes=[sc])
            B.op("scalar", lambda e, t0=t0: e.dma_start(
                s_tm[t0:t0 + 256, :].rearrange("(q p) d -> p q d", p=128), sc.ap),
                reads=[sc], writes=[HB["s_tm"]], dma=True)
            ln_stats(x1, 2, stat)
            def napplyC(e, stat=stat):
                for q in range(2):
                    e.activation(n2.ap[:, q, :], x1.ap[:, q, :], AF.Identity,
                                 bias=stat.ap[:, 24 + q:25 + q],
                                 scale=stat.ap[:, 20 + q:21 + q])
                return e.nop()
            B.op("scalar", napplyC, reads=[x1, stat], writes=[n2])
            if i >= 1:
                emit_fkfr(i - 1)
            for dc in range(DC):
                pb = pool.take()
                transpose_group(None, n2,
                                [n2.ap[:, q, dc * 128:(dc + 1) * 128] for q in range(2)], pb)
                B.op("scalar", lambda e, dc=dc, pb=pb: e.activation(
                    xn2.ap[:, dc, 1:257], pb.ap[:, 0:256],
                    AF.Identity, bias=P["ln2b"].ap[:, dc:dc + 1],
                    scale=P["ln2w"].ap[:, dc:dc + 1]),
                    reads=[pb, P["ln2w"], P["ln2b"]], writes=[xn2])
            for dc in range(DC):
                B.op("vector", lambda e, dc=dc: e.scalar_tensor_tensor(
                    dn2.ap, xn2.ap[:, dc, 1:257], 1.0, xn2.ap[:, dc, 0:256],
                    ALU.mult, ALU.subtract), reads=[xn2], writes=[dn2])

                def mixes2(e, dc=dc, m2k=m2k, m2r=m2r):
                    for tm, m in (("ftmk", m2k), ("ftmr", m2r)):
                        e.scalar_tensor_tensor(
                            m.ap[:, dc, :], dn2.ap, P[tm].ap[:, dc:dc + 1],
                            xn2.ap[:, dc, 0:256], ALU.mult, ALU.add)
                    return e.nop()
                B.op("vector", mixes2, reads=[dn2, xn2, P["ftmk"], P["ftmr"]],
                     writes=[m2k, m2r])
            B.op("gpsimd", lambda e: e.tensor_copy(xn2.ap[:, :, 0:1], xn2.ap[:, :, 256:257]),
                 reads=[xn2], writes=[xn2])
        emit_fkfr(15)
    ar.release()
    B.barrier()

    ar.mark()
    fwv_sb = ar.bf16([128, FC, 1024], "fwv_sb")
    fk2c2 = [ar.bf16([128, FC, 512], f"fk2c{j}") for j in range(2)]
    sfrc = ar.bf16([128, 4, 1024], "sfrc")
    scc = ar.f32([128, 4, 1024], "scc")
    y1p = ar.f32([128, 4, 1024], "y1p")
    y1t = ar.f32([128, 4, 1024], "y1t")
    y1f = ar.bf16([128, DC, 512], "y1f")

    if debug_phases >= 4:
        B.op("gpsimd", lambda e: e.dma_start(
            fwv_sb.ap, fwv_h.rearrange("(c p) o -> p c o", p=128)),
            writes=[fwv_sb], dma=True)
        for i in range(8):
            t0 = 512 * i
            fk2c = fk2c2[i % 2]
            stat = stat2[i % 2]
            B.op("sync", lambda e, t0=t0, fk2c=fk2c: e.dma_start(
                fk2c.ap, fk2_fm.rearrange("(c p) t -> p c t", p=128)[:, :, t0:t0 + 512]),
                reads=[HB["fk2_fm"]], writes=[fk2c], dma=True)
            B.op("sync", lambda e, t0=t0: e.dma_start(
                sfrc.ap, sfr_tm[t0:t0 + 512, :].rearrange("(q p) d -> p q d", p=128)),
                reads=[HB["sfr_tm"]], writes=[sfrc], dma=True)
            B.op("sync", lambda e, t0=t0: e.dma_start(
                scc.ap, s_tm[t0:t0 + 512, :].rearrange("(q p) d -> p q d", p=128)),
                reads=[HB["s_tm"]], writes=[scc], dma=True)
            for q in range(4):
                jobs = []
                for oh in range(2):
                    pb = pool.take()
                    jobs.append((pb, [fk2c.ap[:, fc, q * 128:(q + 1) * 128] for fc in range(FC)],
                                 [fwv_sb.ap[:, fc, oh * 512:(oh + 1) * 512] for fc in range(FC)]))
                mm_jobs(jobs, 512, [fk2c, fwv_sb])
                for oh in range(2):
                    pb = jobs[oh][0]
                    B.op("vector", lambda e, q=q, oh=oh, pb=pb: e.scalar_tensor_tensor(
                        y1p.ap[:, q, oh * 512:(oh + 1) * 512], pb.ap, 1.0,
                        sfrc.ap[:, q, oh * 512:(oh + 1) * 512], ALU.mult, ALU.mult),
                        reads=[pb, sfrc], writes=[y1p])
                    B.op("vector", lambda e, q=q, oh=oh: e.scalar_tensor_tensor(
                        y1p.ap[:, q, oh * 512:(oh + 1) * 512],
                        y1p.ap[:, q, oh * 512:(oh + 1) * 512], 1.0,
                        scc.ap[:, q, oh * 512:(oh + 1) * 512], ALU.mult, ALU.add),
                        reads=[y1p, scc], writes=[y1p])
            ln_stats(y1p, 4, stat)
            def napplyY(e, stat=stat):
                for q in range(4):
                    e.activation(y1t.ap[:, q, :], y1p.ap[:, q, :], AF.Identity,
                                 bias=stat.ap[:, 24 + q:25 + q],
                                 scale=stat.ap[:, 20 + q:21 + q])
                return e.nop()
            B.op("scalar", napplyY, reads=[y1p, stat], writes=[y1t])
            B.op("scalar", lambda e, t0=t0: e.dma_start(
                y1_tm[t0:t0 + 512, :].rearrange("(q p) d -> p q d", p=128), y1t.ap),
                reads=[y1t], writes=[HB["y1_tm"]], dma=True)
            for dc in range(DC):
                pb = pool.take()
                transpose_group(None, y1t,
                                [y1t.ap[:, q, dc * 128:(dc + 1) * 128] for q in range(4)], pb)
                B.op("scalar", lambda e, dc=dc, pb=pb: e.activation(
                    y1f.ap[:, dc, :], pb.ap[:, 0:512], AF.Copy),
                    reads=[pb], writes=[y1f])
            B.op("scalar", lambda e, t0=t0: e.dma_start(
                y1_fm.rearrange("(c p) t -> p c t", p=128)[:, :, t0:t0 + 512], y1f.ap),
                reads=[y1f], writes=[HB["y1_fm"]], dma=True)
    ar.release()
    B.barrier()

    ar.mark()
    wgT = ar.bf16([128, DC, 3072], "wgT")
    wuT = ar.bf16([128, DC, 3072], "wuT")
    y1fc2 = [ar.bf16([128, DC, 512], f"y1fc{j}") for j in range(2)]
    sgb = ar.bf16([128, HC, 512], "sgb")
    hb2_ = [ar.bf16([128, HC, 512], f"hbuf{j}") for j in range(2)]

    if debug_phases >= 5:
        for (srch, hbm, dstb, nch) in ((wg_bf, HB["wg_bf"], wgT, DC),
                                       (wu_bf, HB["wu_bf"], wuT, DC)):
            for c in range(nch):
                B.op("scalar", lambda e, srch=srch, dstb=dstb, c=c: e.dma_start_transpose(
                    dstb.ap[:, c, :], srch[:, c * 128:(c + 1) * 128]),
                    reads=[hbm], writes=[dstb], dma=True)
        for i in range(8):
            t0 = 512 * i
            y1fc, hbv = y1fc2[i % 2], hb2_[i % 2]
            B.op("sync", lambda e, t0=t0, y1fc=y1fc: e.dma_start(
                y1fc.ap, y1_fm.rearrange("(c p) t -> p c t", p=128)[:, :, t0:t0 + 512]),
                reads=[HB["y1_fm"]], writes=[y1fc], dma=True)
            for hh4 in range(0, HC, 2):
                jobs = []
                for hc in (hh4, hh4 + 1):
                    pb = pool.take()
                    jobs.append((pb, [wgT.ap[:, kc, hc * 128:(hc + 1) * 128] for kc in range(DC)],
                                 [y1fc.ap[:, kc, :] for kc in range(DC)]))
                for hc in (hh4, hh4 + 1):
                    pb = pool.take()
                    jobs.append((pb, [wuT.ap[:, kc, hc * 128:(hc + 1) * 128] for kc in range(DC)],
                                 [y1fc.ap[:, kc, :] for kc in range(DC)]))
                mm_jobs(jobs, 512, [wgT, wuT, y1fc])

                def silu2(e, jobs=jobs, hh4=hh4):
                    for k2, hc in enumerate((hh4, hh4 + 1)):
                        e.activation(sgb.ap[:, hc, :], jobs[k2][0].ap, AF.Silu, scale=S127)
                    return e.nop()
                B.op("scalar", silu2, reads=[jobs[0][0], jobs[1][0]], writes=[sgb])

                def hmul2(e, jobs=jobs, hh4=hh4, hbv=hbv):
                    for k2, hc in enumerate((hh4, hh4 + 1)):
                        e.scalar_tensor_tensor(hbv.ap[:, hc, :], jobs[2 + k2][0].ap, 1.0,
                                               sgb.ap[:, hc, :], ALU.mult, ALU.mult)
                    return e.nop()
                B.op("vector", hmul2, reads=[jobs[2][0], jobs[3][0], sgb], writes=[hbv])
            B.op("scalar", lambda e, t0=t0, hbv=hbv: e.dma_start(
                h_fm.rearrange("(c p) t -> p c t", p=128)[:, :, t0:t0 + 512],
                hbv.ap), reads=[hbv], writes=[HB["h_fm"]], dma=True)
    ar.release()
    B.barrier()

    ar.mark()
    wdT = ar.bf16([128, HC, 1024], "wdT")
    hc2 = [ar.bf16([128, HC, 512], f"hc{j}") for j in range(2)]
    y1tc = ar.f32([128, 4, 1024], "y1tc")
    y2p = ar.f32([128, 4, 1024], "y2p")

    if debug_phases >= 5:
        for c in range(HC):
            B.op("scalar", lambda e, c=c: e.dma_start_transpose(
                wdT.ap[:, c, :], wd_bf[:, c * 128:(c + 1) * 128]),
                reads=[HB["wd_bf"]], writes=[wdT], dma=True)
        for i in range(8):
            t0 = 512 * i
            hcv = hc2[i % 2]
            stat = stat2[i % 2]
            B.op("sync", lambda e, t0=t0, hcv=hcv: e.dma_start(
                hcv.ap, h_fm.rearrange("(c p) t -> p c t", p=128)[:, :, t0:t0 + 512]),
                reads=[HB["h_fm"]], writes=[hcv], dma=True)
            B.op("sync", lambda e, t0=t0: e.dma_start(
                y1tc.ap, y1_tm[t0:t0 + 512, :].rearrange("(q p) d -> p q d", p=128)),
                reads=[HB["y1_tm"]], writes=[y1tc], dma=True)
            for qb in range(0, 4, 2):
                jobs = []
                for q in (qb, qb + 1):
                    for oh in range(2):
                        pb = pool.take()
                        jobs.append((pb, [hcv.ap[:, hc, q * 128:(q + 1) * 128] for hc in range(HC)],
                                     [wdT.ap[:, hc, oh * 512:(oh + 1) * 512] for hc in range(HC)]))
                mm_jobs(jobs, 512, [hcv, wdT])
                ji = 0
                for q in (qb, qb + 1):
                    for oh in range(2):
                        pb = jobs[ji][0]; ji += 1
                        B.op("vector", lambda e, q=q, oh=oh, pb=pb: e.scalar_tensor_tensor(
                            y2p.ap[:, q, oh * 512:(oh + 1) * 512], pb.ap, S127 * S127,
                            y1tc.ap[:, q, oh * 512:(oh + 1) * 512], ALU.mult, ALU.add),
                            reads=[pb, y1tc], writes=[y2p])
            ln_stats(y2p, 4, stat)

            def napplyD(e, stat=stat):
                for q in range(4):
                    e.activation(y1tc.ap[:, q, :], y2p.ap[:, q, :], AF.Identity,
                                 bias=stat.ap[:, 24 + q:25 + q],
                                 scale=stat.ap[:, 20 + q:21 + q])
                return e.nop()
            B.op("scalar", napplyD, reads=[y2p, stat], writes=[y1tc])
            B.op("sync", lambda e, t0=t0: e.dma_start(
                out_h[t0:t0 + 512, :].rearrange("(q p) d -> p q d", p=128), y1tc.ap),
                reads=[y1tc], dma=True)
    ar.release()

    for ds in B.dsems["sync"] + B.dsems["scalar"]:
        B.raw("sync", lambda e, ds=ds: e.wait_ge(ds.h, ds.n))
    for ds in B.dsems["gpsimd"]:
        B.raw("gpsimd", lambda e, ds=ds: e.wait_ge(ds.h, ds.n))

    with nc.Block() as blk:
        @blk.sync
        def _(e):
            for fn in B.ops["sync"]:
                fn(e)

        @blk.gpsimd
        def _(e):
            for fn in B.ops["gpsimd"]:
                fn(e)

        @blk.tensor
        def _(e):
            for fn in B.ops["tensor"]:
                fn(e)

        @blk.vector
        def _(e):
            for fn in B.ops["vector"]:
                fn(e)

        @blk.scalar
        def _(e):
            for fn in B.ops["scalar"]:
                fn(e)

    stack.close()
    return nc


_IN_MAP = {
    "wk": "L0_attn_w_key", "wv": "L0_attn_w_value", "wr": "L0_attn_w_recept",
    "wo": "L0_attn_w_output", "fwk": "L0_attn_f_w_key", "fwr": "L0_attn_f_w_recept",
    "fwv": "L0_attn_f_w_value", "wg": "L0_ffn_w_gate", "wu": "L0_ffn_w_up",
    "wd": "L0_ffn_w_down", "ln1w": "L0_attn_ln1_w", "ln1b": "L0_attn_ln1_b",
    "ln2w": "L0_attn_ln2_w", "ln2b": "L0_attn_ln2_b", "tdec": "L0_attn_time_decay",
    "tfir": "L0_attn_time_first", "tmk": "L0_attn_time_mix_k",
    "tmv": "L0_attn_time_mix_v", "tmr": "L0_attn_time_mix_r",
    "ftmk": "L0_attn_f_time_mix_k", "ftmr": "L0_attn_f_time_mix_r",
}


def kernel(**inputs):
    nc = build(DEBUG_PHASES, DEBUG_OUTS)
    shared = {}
    for k, src in _IN_MAP.items():
        a = np.ascontiguousarray(np.asarray(inputs[src]))
        if a.ndim == 3 and a.shape[0] == 1:
            a = a.reshape(-1)
        shared[k] = a
    x_full = np.ascontiguousarray(np.asarray(inputs["L0_attn_x"]))
    in_maps = []
    for b in range(8):
        m = dict(shared)
        m["x"] = x_full[b]
        in_maps.append(m)
    res = run_bass_kernel_spmd(nc, in_maps, core_ids=list(range(8)))
    return np.stack([r["out"] for r in res.results], axis=0)
